# version 10
# baseline (speedup 1.0000x reference)
import numpy as np
from contextlib import ExitStack
import concourse.bass as bass
import concourse.mybir as mybir

F32 = mybir.dt.float32
F16 = mybir.dt.float16
F8 = mybir.dt.float8e4
AF = mybir.ActivationFunctionType
OP = mybir.AluOpType

D = 1024
Hh = 4096
DS = 64
SPAN = 4096
TT = 512
NTILE = SPAN // TT
CH = 128
NCH = TT // CH
KD = D // 128
KH = Hh // 128
LR = float(np.float32(0.01))
INV_D = float(np.float32(1.0) / np.float32(1024.0))
EPS = float(np.float32(1e-5))
W = 8
HTT = TT // 2


def build_program(stage="full", sim1=False):
    nc = bass.Bass("TRN2", target_bir_lowering=False)
    x_d = nc.dram_tensor("x", [KD, 128, SPAN], F32, kind="ExternalInput")
    mk_d = nc.dram_tensor("mask", [KD, 128, SPAN], F16, kind="ExternalInput")
    w1_d = nc.dram_tensor("w1t", [KD, 128, Hh], F32, kind="ExternalInput")
    w2_d = nc.dram_tensor("w2t", [KH, 128, D], F32, kind="ExternalInput")
    wv_d = nc.dram_tensor("wvt", [KD, 128, D], F32, kind="ExternalInput")
    wk_d = nc.dram_tensor("wkt", [KD, 128, DS], F32, kind="ExternalInput")
    wq_d = nc.dram_tensor("wqt", [KD, 128, DS], F32, kind="ExternalInput")
    bh_d = nc.dram_tensor("bias_h", [128, KH], F32, kind="ExternalInput")
    vd_d = nc.dram_tensor("vecd", [128, 24], F32, kind="ExternalInput")
    be_d = nc.dram_tensor("bias_e", [DS, 2], F32, kind="ExternalInput")
    bv_d = nc.dram_tensor("bvrow", [1, D], F32, kind="ExternalInput")
    bk_d = nc.dram_tensor("bkrow", [1, DS], F32, kind="ExternalInput")
    cst_d = nc.dram_tensor("consts", [128, 576], F32, kind="ExternalInput")
    i64_d = nc.dram_tensor("i64h", [DS, DS], F16, kind="ExternalInput")
    lam_d = nc.dram_tensor("lam", [DS, W], F32, kind="ExternalInput")
    lamI_d = nc.dram_tensor("lamI", [DS, W * DS], F16, kind="ExternalInput")
    out_d = nc.dram_tensor("out", [KD, 128, SPAN], F32, kind="ExternalOutput")

    sprp_d = nc.dram_tensor("sp_rp", [KD, 128, SPAN], F16)
    spqh_d = nc.dram_tensor("sp_qh", [NTILE * NCH, DS, CH], F16)
    spct_d = nc.dram_tensor("sp_ct", [NTILE * NCH, DS, D], F16)
    ccin_d = nc.dram_tensor("cc_in", [DS, DS + D], F16)
    ccout_d = nc.dram_tensor("cc_out", [W * DS, DS + D], F16)

    es = ExitStack()
    def sb(name, shape, dt):
        return es.enter_context(nc.sbuf_tensor(name, shape, dt))
    def psum(name, shape, dt=F32):
        return es.enter_context(nc.psum_tensor(name, shape, dt))
    def sem(name):
        return es.enter_context(nc.semaphore(name))

    with es:
        W1Q = sb("W1Q", [128, KD * Hh], F8)
        W2Q = sb("W2Q", [128, KH * D], F8)
        WVT = sb("WVT", [128, KD * D], F16)
        WKT = sb("WKT", [128, KD * DS], F16)
        WQT = sb("WQT", [128, KD * DS], F16)
        B1Q = sb("B1Q", [128, KH], F32)
        B8 = sb("B8", [128, KH], F8)
        B2Q = sb("B2Q", [128, KD], F32)
        B28 = sb("B28", [128, KD], F8)
        GAM = sb("GAM", [128, KD], F32)
        BET = sb("BET", [128, KD], F32)
        BKE = sb("BKE", [DS, 2], F32)
        BVR = sb("BVR", [1, D], F16)
        BKR = sb("BKR", [1, DS], F16)
        ML = sb("ML", [128, 128], F16)
        MU = sb("MU", [128, 128], F16)
        TWOI = sb("TWOI", [128, 128], F16)
        I128F = sb("I128F", [128, 128], F16)
        I64F = sb("I64F", [DS, DS], F32)
        I64H = sb("I64H", [DS, DS], F16)
        LAM = sb("LAM", [DS, W], F32)
        LAMI = sb("LAMI", [DS, W * DS], F16)
        ONE16 = sb("ONE16", [128, 1], F16)
        ONE1F = sb("ONE1F", [1, 128], F32)
        ONE1H = sb("ONE1H", [1, 128], F16)
        EPSA = sb("EPSA", [1, 1], F32)
        ROWS = sb("ROWS", [1, 4 * TT], F32)
        XF = sb("XF", [128, KD * TT], F32)
        XB = sb("XB", [128, KD * TT], F16)
        MKS = sb("MKS", [128, 2 * TT], F16)
        LQ = sb("LQ", [128, KD * TT], F8)
        H8 = sb("H8", [128, KH * TT], F8)
        H16T = sb("H16T", [128, 2 * TT], F16)
        Y16T = sb("Y16T", [128, 2 * TT], F16)
        F32T = sb("F32T", [128, TT], F32)
        TMPA = sb("TMPA", [128, 2 * CH], F16)
        TMPB = sb("TMPB", [128, 2 * CH], F16)
        TMPF = sb("TMPF", [128, 2 * HTT], F32)
        RBF = sb("RBF", [128, 2 * HTT], F32)
        RB16 = sb("RB16", [128, 2], F16)
        HA = sb("HA", [128, KD * TT], F16)
        RP = sb("RP", [128, KD * TT], F16)
        VSB = sb("VSB", [128, 2 * D], F16)
        KSB = sb("KSB", [DS, 4 * CH], F16)
        QSB = sb("QSB", [DS, 4 * CH], F16)
        KTM = sb("KTM", [128, 4 * DS], F16)
        TB = sb("TB", [128, 4 * 128], F16)
        MSB = sb("MSB", [128, 4 * 128], F16)
        MTSB = sb("MTSB", [128, 4 * 128], F16)
        HB = sb("HB", [128, 4 * 128], F16)
        WTT = sb("WTT", [128, 2 * 128], F16)
        UV = sb("UV", [128, 2 * D], F16)
        UK = sb("UK", [128, 2 * DS], F16)
        QH = sb("QH", [DS, CH], F16)
        PC = sb("PC", [DS, NTILE * NCH * DS], F16)
        CTC = sb("CTC", [DS, D], F16)
        SPB = sb("SPB", [DS, DS], F16)
        SPTB = sb("SPTB", [DS, DS], F16)
        SCTB = sb("SCTB", [DS, D], F16)
        ZSB = sb("ZSB", [DS, D], F16)
        ZC1 = sb("ZC1", [DS, D], F16)
        ZH = sb("ZH", [DS, (NCH - 1) * D], F16)
        QHL = sb("QHL", [DS, NCH * CH], F16)
        CTL = sb("CTL", [DS, 2 * D], F16)
        GB = CTL[0:DS, 0:DS + D]
        PBL = QH[0:DS, 0:DS]
        CBL = CTC

        pA = [psum("pA0", [128, TT]), psum("pA1", [128, TT])]
        pB = [psum("pB0", [128, TT]), psum("pB1", [128, TT])]
        pC = psum("pC", [128, D])
        pD = psum("pD", [128, 512])
        pE = psum("pE", [128, 512])
        pE2A = pD[:, 384:512]

        s_w = sem("s_w"); s_wg = sem("s_wg"); s_x = sem("s_x")
        s_x0 = sem("s_x0"); s_x1 = sem("s_x1"); s_x2 = sem("s_x2"); s_x3 = sem("s_x3")
        s_sp = sem("s_sp"); s_r = sem("s_r")
        s_rc0 = sem("s_rc0"); s_rc1 = sem("s_rc1")
        s_mk0 = sem("s_mk0"); s_mk1 = sem("s_mk1")
        s_mk2 = sem("s_mk2"); s_mk3 = sem("s_mk3")
        s_o = sem("s_o"); s_cc = sem("s_cc")
        s_t = sem("s_t"); s_v = sem("s_v"); s_s = sem("s_s"); s_g = sem("s_g")

        cnt = {"t": 0, "v": 0, "s": 0, "g": 0, "w": 0, "wg": 0, "x": 0, "sp": 0,
               "r": 0, "o": 0, "rc0": 0, "rc1": 0, "mk0": 0, "mk1": 0,
               "mk2": 0, "mk3": 0, "x0": 0, "x1": 0, "x2": 0, "x3": 0}
        SEMS = {"t": s_t, "v": s_v, "s": s_s, "g": s_g, "w": s_w, "wg": s_wg,
                "x": s_x, "sp": s_sp, "r": s_r, "o": s_o, "rc0": s_rc0, "rc1": s_rc1,
                "mk0": s_mk0, "mk1": s_mk1, "mk2": s_mk2, "mk3": s_mk3,
                "x0": s_x0, "x1": s_x1, "x2": s_x2, "x3": s_x3}

        def tick(instr, key, n=1):
            cnt[key] += n
            instr.then_inc(SEMS[key], n)
            return cnt[key]

        blk = es.enter_context(nc.Block())

        @blk.gpsimd
        def _(g):
            for kd in range(KD):
                tick(g.dma_start(out=W1Q[:, kd * Hh:(kd + 1) * Hh], in_=w1_d[kd]), "wg", 16)
            for kh in range(KH):
                if kh == 16:
                    g.wait_ge(s_wg, cnt["wg"])
                tick(g.dma_start(out=W2Q[:, kh * D:(kh + 1) * D], in_=w2_d[kh]), "wg", 16)
            g.wait_ge(s_wg, cnt["wg"])
            for kd in range(KD):
                tick(g.dma_start(out=WVT[:, kd * D:(kd + 1) * D], in_=wv_d[kd]), "wg", 16)
            for kd in range(KD):
                tick(g.dma_start(out=WKT[:, kd * DS:(kd + 1) * DS], in_=wk_d[kd]), "wg", 16)
                tick(g.dma_start(out=WQT[:, kd * DS:(kd + 1) * DS], in_=wq_d[kd]), "wg", 16)
            tick(g.dma_start(out=B8[:], in_=bh_d[:]), "wg", 16)
            tick(g.dma_start(out=B28[:], in_=vd_d[:, 0:KD]), "wg", 16)
            tick(g.dma_start(out=BVR[:], in_=bv_d[:]), "wg", 16)
            tick(g.dma_start(out=BKR[:], in_=bk_d[:]), "wg", 16)
            tick(g.dma_start(out=LAMI[:], in_=lamI_d[:]), "wg", 16)
            tick(g.dma_start(out=ML[:], in_=cst_d[:, 0:128]), "wg", 16)
            tick(g.dma_start(out=MU[:], in_=cst_d[:, 128:256]), "wg", 16)
            tick(g.dma_start(out=TWOI[:], in_=cst_d[:, 256:384]), "wg", 16)
            tick(g.dma_start(out=I128F[:], in_=cst_d[:, 384:512]), "wg", 16)

        @blk.sync
        def _(sy):
            tick(sy.dma_start(out=GAM[:], in_=vd_d[:, 8:16]), "w", 16)
            tick(sy.dma_start(out=BET[:], in_=vd_d[:, 16:24]), "w", 16)
            tick(sy.dma_start(out=BKE[:], in_=be_d[:]), "w", 16)
            tick(sy.dma_start(out=I64F[:], in_=cst_d[0:DS, 512:576]), "w", 16)
            tick(sy.dma_start(out=I64H[:], in_=i64_d[:]), "w", 16)
            tick(sy.dma_start(out=LAM[:], in_=lam_d[:]), "w", 16)
        w_done = cnt["w"]
        wg_done = cnt["wg"]

        @blk.scalar
        def _(s):
            s.preload_act_table(AF.Sqrt)

        @blk.vector
        def _(v):
            v.wait_ge(s_w, w_done)
            v.wait_ge(s_wg, wg_done)
            v.memset(ONE16[:], 1.0)
            v.memset(RB16[:], 0.0)
            v.memset(ONE1F[:], 1.0)
            v.memset(ONE1H[:], 1.0)
            v.memset(EPSA[:], EPS)
            v.tensor_copy(B1Q[:], B8[:])
            v.tensor_copy(B2Q[:], B28[:])
            v.tensor_copy(SPB[:], I64H[:])
            v.tensor_copy(SPTB[:], I64H[:])
            tick(v.memset(SCTB[:], 0.0), "v")
        init_v = cnt["v"]

        rel = {"LQ0": ("t", 0), "LQ1": ("t", 0), "XB": ("t", 0),
               "H80": ("t", 0), "H81": ("t", 0)}

        pe_guard = {"v": 0, "s": 0}
        rbf_guard = [0]
        rows_guard = {0: ("t", 0), 1: ("t", 0)}
        xb_guard = [0]

        def rslot(p, which, hh=None, half=False):
            base = (2 * p + (1 if which == "C" else 0)) * TT
            if half:
                return ROWS[0:1, base + hh * HTT: base + (hh + 1) * HTT]
            return ROWS[0:1, base: base + TT]

        def emit_xb_dma(tt, xb_free):
            @blk.gpsimd
            def _(g):
                for k, val in xb_free:
                    g.wait_ge(SEMS[k], val)
                for kd in range(KD):
                    tick(g.dma_start(out=XB[:, kd * TT:(kd + 1) * TT],
                                     in_=x_d[kd, :, tt * TT:(tt + 1) * TT]), "wg", 16)
            return cnt["wg"]

        def emit_ln_stats_a(waits, src16, rpair, copy_src=None, pump=None):
            ST = {"rpair": rpair, "s_bc0": None, "v_ap0": None, "src16": src16}
            cw = []
            if copy_src is not None:
                @blk.scalar
                def _(s):
                    for k, val in waits:
                        s.wait_ge(SEMS[k], val)
                    for kd in range(KD):
                        i = s.activation(XB[:, kd * TT:(kd + 1) * TT],
                                         copy_src[:, kd * TT:(kd + 1) * TT],
                                         AF.Copy, bias=0.0, scale=1.0)
                    tick(i, "s")
                cw = [("s", cnt["s"])]
                if pump: pump()
            @blk.tensor
            def _(t):
                for k, val in waits:
                    t.wait_ge(SEMS[k], val)
                for k, val in cw:
                    t.wait_ge(SEMS[k], val)
                t.wait_ge(s_v, pe_guard["v"])
                t.wait_ge(s_s, pe_guard["s"])
                for kd in range(KD):
                    i = t.matmul(pE[0:1, 0:TT], ONE16[:],
                                 src16[:, kd * TT:(kd + 1) * TT],
                                 start=(kd == 0), stop=(kd == KD - 1))
                tick(i, "t")
            ST["t_sum"] = cnt["t"]
            if pump: pump()
            @blk.scalar
            def _(s):
                s.wait_ge(s_t, max(ST["t_sum"], xb_guard[0]))
                for kd in range(KD):
                    i = s.activation(XB[:, kd * TT:(kd + 1) * TT],
                                     src16[:, kd * TT:(kd + 1) * TT],
                                     AF.Square, bias=0.0, scale=float(1.0 / 64.0))
                tick(i, "s")
            ST["s_sq"] = cnt["s"]
            return ST

        def emit_ln_stats_b(ST, pump=None):
            rpair = ST["rpair"]
            @blk.tensor
            def _(t):
                t.wait_ge(s_s, ST["s_sq"])
                for kd in range(KD):
                    i = t.matmul(pE[32:33, 0:TT], ONE16[:],
                                 XB[:, kd * TT:(kd + 1) * TT],
                                 start=(kd == 0), stop=(kd == KD - 1))
                tick(i, "t")
            ST["t_sq"] = cnt["t"]
            xb_guard[0] = cnt["t"]
            if pump: pump()
            B = rslot(rpair, "B")
            C = rslot(rpair, "C")
            rg = rows_guard[rpair]
            @blk.vector
            def _(v):
                v.wait_ge(s_t, ST["t_sq"])
                v.wait_ge(SEMS[rg[0]], rg[1])
                tick(v.tensor_scalar(B, pE[0:1, 0:TT], INV_D, None, OP.mult), "v")
                v.wait_ge(s_v, cnt["v"])
                tick(v.tensor_tensor(C, B, B, OP.mult), "v")
                v.wait_ge(s_v, cnt["v"])
                tick(v.scalar_tensor_tensor(B, pE[32:33, 0:TT], 4.0, C,
                                            OP.mult, OP.subtract), "v")
            v_var = cnt["v"]
            @blk.scalar
            def _(s):
                s.wait_ge(s_v, v_var)
                tick(s.activation(B, B, AF.Sqrt, bias=EPSA[:, 0:1]), "s")
            s_sd = cnt["s"]
            @blk.vector
            def _(v):
                v.wait_ge(s_s, s_sd)
                tick(v.reciprocal(C, B), "v")
                v.wait_ge(s_v, cnt["v"])
                tick(v.scalar_tensor_tensor(B, pE[0:1, 0:TT], INV_D, C,
                                            OP.mult, OP.mult), "v")
            ST["v_rows"] = cnt["v"]
            pe_guard["v"] = cnt["v"]
            if pump: pump()
            return ST

        def emit_ln_stats(waits, src16, rpair, copy_src=None, pump=None):
            ST = emit_ln_stats_a(waits, src16, rpair, copy_src, pump)
            return emit_ln_stats_b(ST, pump)

        def emit_ln_tail(out_kind, hh, ST, src, apply_waits=(), pair_waits=None,
                         pair_ticks=None):
            lq_rel = rel["LQ%d" % hh]
            rp = ST["rpair"]
            rbg = rbf_guard[0]
            @blk.tensor
            def _(t):
                if hh == 0:
                    t.wait_ge(s_v, ST["v_rows"])
                    t.wait_ge(s_s, pe_guard["s"])
                else:
                    t.wait_ge(s_s, ST["s_bc0"])
                t.matmul(pE[:, 0:HTT], ONE1F[:], rslot(rp, "C", hh, True),
                         start=True, stop=True)
                tick(t.matmul(pE[:, HTT:2 * HTT], ONE1F[:], rslot(rp, "B", hh, True),
                              start=True, stop=True), "t")
            t_bc = cnt["t"]
            rows_guard[rp] = ("t", t_bc)

            @blk.scalar
            def _(s):
                s.wait_ge(s_t, t_bc)
                if hh == 0:
                    s.wait_ge(s_v, rbg)
                else:
                    s.wait_ge(s_v, ST["v_ap0"])
                s.activation(RBF[:, 0:HTT], pE[:, 0:HTT], AF.Copy,
                             bias=0.0, scale=1.0)
                tick(s.activation(RBF[:, HTT:2 * HTT], pE[:, HTT:2 * HTT],
                                  AF.Copy, bias=0.0, scale=1.0), "s")
            s_bc = cnt["s"]
            pe_guard["s"] = s_bc
            if hh == 0:
                ST["s_bc0"] = s_bc

            @blk.vector
            def _(v):
                v.wait_ge(s_s, s_bc)
                if out_kind == "q":
                    v.wait_ge(SEMS[lq_rel[0]], lq_rel[1])
                for k, val in apply_waits:
                    v.wait_ge(SEMS[k], val)
                rstd = RBF[:, 0:HTT]
                mr = RBF[:, HTT:2 * HTT]
                tf = [TMPF[:, 0:HTT], TMPF[:, HTT:2 * HTT]]
                for kp in range(KD // 2):
                    kk = [2 * kp, 2 * kp + 1]
                    sls = [slice(kk[i] * TT + hh * HTT,
                                 kk[i] * TT + (hh + 1) * HTT) for i in (0, 1)]
                    if pair_waits is not None:
                        for k, val in pair_waits[kp]:
                            v.wait_ge(SEMS[k], val)
                    if kp > 0 or hh == 1:
                        v.wait_ge(s_v, cnt["v"])
                    v.tensor_tensor(tf[0], src[:, sls[0]], rstd, OP.mult)
                    tick(v.tensor_tensor(tf[1], src[:, sls[1]], rstd, OP.mult), "v")
                    v.wait_ge(s_v, cnt["v"])
                    if out_kind == "q":
                        v.tensor_tensor(LQ[:, sls[0]], tf[0], mr, OP.subtract)
                        tick(v.tensor_tensor(LQ[:, sls[1]], tf[1], mr,
                                             OP.subtract), "v")
                    else:
                        v.tensor_tensor(tf[0], tf[0], mr, OP.subtract)
                        tick(v.tensor_tensor(tf[1], tf[1], mr, OP.subtract), "v")
                        v.wait_ge(s_v, cnt["v"])
                        for i in (0, 1):
                            v.tensor_tensor(RP[:, sls[i]], src[:, sls[i]], tf[i],
                                            OP.add)
                        v.tensor_scalar(HA[:, sls[0]], tf[0],
                                        GAM[:, kk[0]:kk[0] + 1],
                                        BET[:, kk[0]:kk[0] + 1], OP.mult, OP.add)
                        tick(v.tensor_scalar(HA[:, sls[1]], tf[1],
                                             GAM[:, kk[1]:kk[1] + 1],
                                             BET[:, kk[1]:kk[1] + 1],
                                             OP.mult, OP.add), "v")
                    if pair_ticks is not None:
                        pair_ticks[kp] = cnt["v"]
            v_apply = cnt["v"]
            rbf_guard[0] = v_apply
            if hh == 0:
                ST["v_ap0"] = v_apply
            return v_apply

        mk_guard = {0: ("v", 0), 1: ("v", 0)}
        def emit_ffn(tt, ln_v, res_src, res_inplace, evac_waits=(), pump=None,
                     mid_hook=None, mid_hook2=None):
            s_preh = cnt["s"]
            h8_rel = rel["H80"]
            tm = {}; sm = {}; vm = {}
            for kh in range(KH):
                @blk.tensor
                def _(t):
                    if kh == 0:
                        t.wait_ge(s_v, ln_v)
                    if kh < 2:
                        t.wait_ge(s_s, s_preh)
                    else:
                        t.wait_ge(s_s, sm[kh - 2])
                    bank = pA[kh % 2]
                    for kd in range(KD):
                        i = t.matmul(bank[:, 0:TT],
                                     W1Q[:, kd * Hh + kh * 128: kd * Hh + (kh + 1) * 128],
                                     LQ[:, kd * TT:(kd + 1) * TT],
                                     start=(kd == 0), stop=(kd == KD - 1))
                    tick(i, "t")
                tm[kh] = cnt["t"]
                @blk.scalar
                def _(s):
                    s.wait_ge(s_t, tm[kh])
                    if kh >= 2:
                        s.wait_ge(s_v, vm[kh - 2])
                    tick(s.activation(H16T[:, (kh % 2) * TT:(kh % 2 + 1) * TT],
                                      pA[kh % 2][:, 0:TT], AF.Relu,
                                      bias=B1Q[:, kh:kh + 1], scale=1.0), "s")
                sm[kh] = cnt["s"]
                @blk.vector
                def _(v):
                    if kh == 0:
                        v.wait_ge(SEMS[h8_rel[0]], h8_rel[1])
                    v.wait_ge(s_s, sm[kh])
                    tick(v.tensor_copy(H8[:, kh * TT:(kh + 1) * TT],
                                       H16T[:, (kh % 2) * TT:(kh % 2 + 1) * TT]), "v")
                vm[kh] = cnt["v"]
                if pump:
                    pump()
                if kh == 20 and mid_hook:
                    mid_hook()
            rel["LQ0"] = rel["LQ1"] = ("t", tm[KH - 1])

            tm2 = {}; sm2 = {}; vm2 = {}
            def emit_add(v, pd):
                sl = slice(pd * TT, (pd + 1) * TT)
                pft = F32T[:, 0:TT]
                if res_inplace:
                    return v.tensor_tensor(XF[:, sl], XF[:, sl], pft, OP.add)
                return v.tensor_tensor(XF[:, sl], res_src[:, sl], pft, OP.add)
            for dv in range(KD):
                p2 = dv % 2
                mkkey = "mk%d" % p2
                g0 = mk_guard[p2]
                @blk.sync
                def _(sy):
                    sy.wait_ge(SEMS[g0[0]], g0[1])
                    tick(sy.dma_start(out=MKS[:, p2 * TT:(p2 + 1) * TT],
                                      in_=mk_d[dv, :, tt * TT:(tt + 1) * TT]), mkkey, 16)
                mk_ld = cnt[mkkey]
                @blk.tensor
                def _(t):
                    if dv == 0:
                        t.wait_ge(s_v, vm[KH - 1])
                    if dv < 2:
                        t.wait_ge(s_s, s_preh)
                    else:
                        t.wait_ge(s_s, sm2[dv - 2])
                    bank = pB[p2]
                    for kh in range(KH):
                        i = t.matmul(bank[:, 0:TT],
                                     W2Q[:, kh * D + dv * 128: kh * D + (dv + 1) * 128],
                                     H8[:, kh * TT:(kh + 1) * TT],
                                     start=(kh == 0), stop=(kh == KH - 1))
                    tick(i, "t")
                tm2[dv] = cnt["t"]
                @blk.scalar
                def _(s):
                    s.wait_ge(s_t, tm2[dv])
                    if dv >= 2:
                        s.wait_ge(s_v, vm2[dv - 2])
                    tick(s.activation(Y16T[:, p2 * TT:(p2 + 1) * TT],
                                      pB[p2][:, 0:TT], AF.Identity,
                                      bias=B2Q[:, dv:dv + 1], scale=1.0), "s")
                sm2[dv] = cnt["s"]
                @blk.vector
                def _(v):
                    if dv == 0:
                        for k, val in evac_waits:
                            v.wait_ge(SEMS[k], val)
                    v.wait_ge(s_s, sm2[dv])
                    v.wait_ge(SEMS[mkkey], mk_ld)
                    if dv > 0:
                        v.wait_ge(s_v, vm2[dv - 1])
                        tick(emit_add(v, dv - 1), "v")
                        v.wait_ge(s_v, cnt["v"])
                    tick(v.tensor_tensor(F32T[:, 0:TT],
                                         Y16T[:, p2 * TT:(p2 + 1) * TT],
                                         MKS[:, p2 * TT:(p2 + 1) * TT],
                                         OP.mult), "v")
                vm2[dv] = cnt["v"]
                mk_guard[p2] = ("v", cnt["v"])
                if pump:
                    pump()
                if dv == 1 and mid_hook2:
                    mid_hook2()
            @blk.vector
            def _(v):
                v.wait_ge(s_v, vm2[KD - 1])
                tick(emit_add(v, KD - 1), "v")
            rel["H80"] = rel["H81"] = ("t", tm2[KD - 1])
            return cnt["v"]

        qhct_prev = [0]
        rp_prev = [0]
        span_s = [0]
        rp_tile_spill = [0] * NTILE

        def quad_stages_p1(tt, ha0, ha1):
            st = {}
            csls = [[slice(kd * TT + ch * CH, kd * TT + (ch + 1) * CH)
                     for kd in range(KD)] for ch in range(NCH)]
            for g in (0, 1):
                @blk.tensor
                def _(t, g=g):
                    if g == 0:
                        t.wait_ge(s_v, ha0)
                        if span_s[0]:
                            t.wait_ge(s_s, span_s[0])
                    else:
                        t.wait_ge(s_v, ha1)
                        t.wait_ge(s_s, st[("s_kq", 0)])
                    for jj in (0, 1):
                        j = 2 * g + jj
                        for kd in range(KD):
                            t.matmul(pD[0:DS, jj * CH:(jj + 1) * CH],
                                     WKT[:, kd * DS:(kd + 1) * DS],
                                     HA[:, csls[j][kd]],
                                     start=(kd == 0), stop=(kd == KD - 1))
                        for kd in range(KD):
                            t.matmul(pD[0:DS, 256 + jj * CH:256 + (jj + 1) * CH],
                                     WQT[:, kd * DS:(kd + 1) * DS],
                                     HA[:, csls[j][kd]],
                                     start=(kd == 0), stop=(kd == KD - 1))
                        for kd in range(KD):
                            t.matmul(pC[0:CH, jj * DS:(jj + 1) * DS],
                                     HA[:, csls[j][kd]],
                                     WKT[:, kd * DS:(kd + 1) * DS],
                                     start=(kd == 0), stop=False)
                        i = t.matmul(pC[0:CH, jj * DS:(jj + 1) * DS], ONE1H[:],
                                     BKR[:], start=False, stop=True)
                    tick(i, "t")
                st[("t_kq", g)] = cnt["t"]
                yield
                @blk.scalar
                def _(s, g=g):
                    s.wait_ge(s_t, st[("t_kq", g)])
                    for jj in (0, 1):
                        j = 2 * g + jj
                        s.activation(KSB[:, j * CH:(j + 1) * CH],
                                     pD[0:DS, jj * CH:(jj + 1) * CH],
                                     AF.Identity, bias=BKE[:, 0:1], scale=1.0)
                        s.activation(QSB[:, j * CH:(j + 1) * CH],
                                     pD[0:DS, 256 + jj * CH:256 + (jj + 1) * CH],
                                     AF.Identity, bias=BKE[:, 1:2], scale=1.0)
                        i = s.activation(KTM[:, j * DS:(j + 1) * DS],
                                         pC[0:CH, jj * DS:(jj + 1) * DS],
                                         AF.Copy, bias=0.0, scale=1.0)
                    tick(i, "s")
                st[("s_kq", g)] = cnt["s"]
                yield
            @blk.tensor
            def _(t):
                t.wait_ge(s_s, st[("s_kq", 1)])
                for j in range(4):
                    i = t.matmul(pD[:, j * 128:(j + 1) * 128],
                                 KSB[:, j * CH:(j + 1) * CH],
                                 KSB[:, j * CH:(j + 1) * CH],
                                 start=True, stop=True)
                tick(i, "t")
            st["t_A"] = cnt["t"]
            yield
            for g in (0, 1):
                @blk.vector
                def _(v, g=g):
                    if g == 0:
                        v.wait_ge(s_t, st["t_A"])
                    else:
                        v.wait_ge(s_v, cnt["v"])
                    for jj in (0, 1):
                        j = 2 * g + jj
                        sl1 = slice(jj * 128, (jj + 1) * 128)
                        v.tensor_scalar(TMPA[:, sl1], pD[:, j * 128:(j + 1) * 128],
                                        LR, None, OP.mult)
                    tick(v.tensor_copy(RB16[:, 0:1], RB16[:, 0:1]), "v")
                    v.wait_ge(s_v, cnt["v"])
                    for jj in (0, 1):
                        sl1 = slice(jj * 128, (jj + 1) * 128)
                        v.tensor_tensor(TMPB[:, sl1], TMPA[:, sl1], ML[:], OP.mult)
                        v.tensor_tensor(TMPA[:, sl1], TMPA[:, sl1], MU[:], OP.mult)
                    tick(v.tensor_copy(RB16[:, 0:1], RB16[:, 0:1]), "v")
                    v.wait_ge(s_v, cnt["v"])
                    for jj in (0, 1):
                        j = 2 * g + jj
                        sl1 = slice(jj * 128, (jj + 1) * 128)
                        jsl = slice(j * 128, (j + 1) * 128)
                        v.tensor_tensor(TB[:, jsl], TMPB[:, sl1], I128F[:], OP.add)
                        v.tensor_tensor(MSB[:, jsl], I128F[:], TMPA[:, sl1],
                                        OP.subtract)
                        i = v.tensor_tensor(MTSB[:, jsl], I128F[:], TMPB[:, sl1],
                                            OP.subtract)
                    tick(i, "v")
                if g == 0:
                    yield
            prev = ("v", cnt["v"])
            yield
            for it in range(6):
                @blk.tensor
                def _(t):
                    t.wait_ge(SEMS[prev[0]], prev[1])
                    for j in range(4):
                        jsl = slice(j * 128, (j + 1) * 128)
                        i = t.matmul(pD[:, jsl], TB[:, jsl], MSB[:, jsl],
                                     start=True, stop=True)
                    tick(i, "t")
                st["t_G"] = cnt["t"]
                yield
                @blk.scalar
                def _(s):
                    s.wait_ge(s_t, st["t_G"])
                    for j in range(4):
                        jsl = slice(j * 128, (j + 1) * 128)
                        i = s.activation(HB[:, jsl], pD[:, jsl], AF.Copy,
                                         bias=0.0, scale=-1.0)
                    tick(i, "s")
                st["s_Gn"] = cnt["s"]
                @blk.tensor
                def _(t):
                    t.wait_ge(s_s, st["s_Gn"])
                    for j in range(4):
                        jsl = slice(j * 128, (j + 1) * 128)
                        j2 = slice(512 + j * 128, 512 + (j + 1) * 128)
                        t.matmul(pC[:, jsl], TWOI[:], MSB[:, jsl],
                                 start=True, stop=False)
                        t.matmul(pC[:, jsl], MTSB[:, jsl], HB[:, jsl],
                                 start=False, stop=True)
                        t.matmul(pC[:, j2], TWOI[:], MTSB[:, jsl],
                                 start=True, stop=False)
                        i = t.matmul(pC[:, j2], HB[:, jsl], MTSB[:, jsl],
                                     start=False, stop=True)
                    tick(i, "t")
                st["t_M"] = cnt["t"]
                yield
                @blk.scalar
                def _(s):
                    s.wait_ge(s_t, st["t_M"])
                    for j in range(4):
                        jsl = slice(j * 128, (j + 1) * 128)
                        j2 = slice(512 + j * 128, 512 + (j + 1) * 128)
                        s.activation(MSB[:, jsl], pC[:, jsl], AF.Copy,
                                     bias=0.0, scale=1.0)
                        i = s.activation(MTSB[:, jsl], pC[:, j2], AF.Copy,
                                         bias=0.0, scale=1.0)
                    tick(i, "s")
                prev = ("s", cnt["s"])
            yield
            for j in range(4):
                par = j % 2
                cidx = tt * NCH + j
                @blk.tensor
                def _(t, j=j, par=par):
                    t.wait_ge(SEMS[prev[0]], prev[1])
                    if j > 0:
                        t.wait_ge(s_s, span_s[0])
                    for half in range(2):
                        for kd in range(KD):
                            t.matmul(pC[0:CH, half * 512:(half + 1) * 512],
                                     HA[:, csls[j][kd]],
                                     WVT[:, kd * D + half * 512:
                                          kd * D + (half + 1) * 512],
                                     start=(kd == 0), stop=False)
                        i = t.matmul(pC[0:CH, half * 512:(half + 1) * 512],
                                     ONE1H[:], BVR[0:1, half * 512:(half + 1) * 512],
                                     start=False, stop=True)
                    tick(i, "t")
                st[("t_v", j)] = cnt["t"]
                yield
                @blk.scalar
                def _(s, j=j, par=par):
                    s.wait_ge(s_t, st[("t_v", j)])
                    if j >= 2:
                        s.wait_ge(s_t, st[("t_U", j - 2)])
                    s.activation(VSB[:, par * D:par * D + 512], pC[0:CH, 0:512],
                                 AF.Copy, bias=0.0, scale=1.0)
                    tick(s.activation(VSB[:, par * D + 512:par * D + 1024],
                                      pC[0:CH, 512:1024], AF.Copy,
                                      bias=0.0, scale=1.0), "s")
                st[("s_v", j)] = cnt["s"]
                yield
                @blk.tensor
                def _(t, j=j, par=par):
                    t.wait_ge(s_s, st[("s_v", j)])
                    m = MSB[:, j * 128:(j + 1) * 128]
                    t.matmul(pC[0:CH, 0:512], m, VSB[:, par * D:par * D + 512],
                             start=True, stop=True)
                    t.matmul(pC[0:CH, 512:1024], m,
                             VSB[:, par * D + 512:par * D + 1024],
                             start=True, stop=True)
                    t.matmul(pD[0:CH, 0:DS], m, KTM[:, j * DS:(j + 1) * DS],
                             start=True, stop=True)
                    tick(t.matmul(pD[:, 256 + par * 128:384 + par * 128],
                                  KSB[:, j * CH:(j + 1) * CH],
                                  QSB[:, j * CH:(j + 1) * CH],
                                  start=True, stop=True), "t")
                st[("t_U", j)] = cnt["t"]
                yield
                @blk.scalar
                def _(s, j=j, par=par):
                    s.wait_ge(s_t, st[("t_U", j)])
                    s.activation(UV[:, par * D:par * D + 512], pC[0:CH, 0:512],
                                 AF.Copy, bias=0.0, scale=LR)
                    s.activation(UV[:, par * D + 512:par * D + 1024],
                                 pC[0:CH, 512:1024], AF.Copy, bias=0.0, scale=LR)
                    tick(s.activation(UK[:, par * DS:(par + 1) * DS],
                                      pD[0:CH, 0:DS], AF.Copy, bias=0.0,
                                      scale=LR), "s")
                st[("s_U", j)] = cnt["s"]
                @blk.vector
                def _(v, j=j, par=par):
                    v.wait_ge(s_t, st[("t_U", j)])
                    tick(v.tensor_tensor(WTT[:, par * 128:(par + 1) * 128],
                                         pD[:, 256 + par * 128:384 + par * 128],
                                         MU[:], OP.mult), "v")
                st[("v_W", j)] = cnt["v"]
                yield
                for wave in range(2):
                    @blk.tensor
                    def _(t, j=j, par=par, wave=wave):
                        t.wait_ge(s_s, st[("s_U", j)])
                        t.wait_ge(s_v, st[("v_W", j)] if wave == 0
                                  else st[("v_y", j)])
                        for q in range(4):
                            kd = wave * 4 + q
                            i = t.matmul(pC[:, q * 128:(q + 1) * 128],
                                         UV[:, par * D + kd * 128:
                                             par * D + (kd + 1) * 128],
                                         WTT[:, par * 128:(par + 1) * 128],
                                         start=True, stop=True)
                        tick(i, "t")
                    st[("t_y", j)] = cnt["t"]
                    yield
                    @blk.vector
                    def _(v, j=j, wave=wave):
                        v.wait_ge(s_t, st[("t_y", j)])
                        v.wait_ge(s_sp, rp_prev[0])
                        for q in range(4):
                            kd = wave * 4 + q
                            i = v.tensor_tensor(RP[:, csls[j][kd]],
                                                RP[:, csls[j][kd]],
                                                pC[:, q * 128:(q + 1) * 128],
                                                OP.add)
                        tick(i, "v")
                    st[("v_y", j)] = cnt["v"]
                yield
                @blk.tensor
                def _(t, j=j, par=par):
                    t.wait_ge(s_v, st[("v_y", j)])
                    if span_s[0]:
                        t.wait_ge(s_s, span_s[0])
                    t.matmul(pD[0:DS, 0:128], UK[:, par * DS:(par + 1) * DS],
                             WTT[:, par * 128:(par + 1) * 128],
                             start=True, stop=True)
                    t.matmul(pD[0:DS, 128:128 + DS], UK[:, par * DS:(par + 1) * DS],
                             KTM[:, j * DS:(j + 1) * DS], start=True, stop=True)
                    t.matmul(pC[0:DS, 0:512], KTM[:, j * DS:(j + 1) * DS],
                             UV[:, par * D:par * D + 512], start=True, stop=True)
                    tick(t.matmul(pC[0:DS, 512:1024], KTM[:, j * DS:(j + 1) * DS],
                                  UV[:, par * D + 512:par * D + 1024],
                                  start=True, stop=True), "t")
                st[("t_QP", j)] = cnt["t"]
                yield
                @blk.vector
                def _(v, j=j, par=par, cidx=cidx):
                    v.wait_ge(s_t, st[("t_QP", j)])
                    v.wait_ge(s_sp, qhct_prev[0])
                    v.scalar_tensor_tensor(QH[:], pD[0:DS, 0:128], -1.0,
                                           QSB[:, j * CH:(j + 1) * CH],
                                           OP.mult, OP.add)
                    tick(v.scalar_tensor_tensor(PC[:, cidx * DS:(cidx + 1) * DS],
                                                pD[0:DS, 128:128 + DS], -1.0,
                                                I64F[:], OP.mult, OP.add), "v")
                st[("v_QP", j)] = cnt["v"]
                @blk.scalar
                def _(s, j=j):
                    s.wait_ge(s_t, st[("t_QP", j)])
                    s.wait_ge(s_sp, qhct_prev[0])
                    s.activation(CTC[:, 0:512], pC[0:DS, 0:512], AF.Copy,
                                 bias=0.0, scale=1.0)
                    tick(s.activation(CTC[:, 512:1024], pC[0:DS, 512:1024],
                                      AF.Copy, bias=0.0, scale=1.0), "s")
                st[("s_ctc", j)] = cnt["s"]
                yield
                @blk.gpsimd
                def _(g, j=j, cidx=cidx):
                    g.wait_ge(s_v, st[("v_QP", j)])
                    g.wait_ge(s_s, st[("s_ctc", j)])
                    g.wait_ge(s_sp, cnt["sp"])
                    tick(g.dma_start(out=spqh_d[cidx], in_=QH[:]), "sp", 16)
                    tick(g.dma_start(out=spct_d[cidx], in_=CTC[:]), "sp", 16)
                qhct_prev[0] = cnt["sp"]
                @blk.tensor
                def _(t, j=j, cidx=cidx):
                    t.wait_ge(s_v, st[("v_QP", j)])
                    t.wait_ge(s_s, st[("s_ctc", j)])
                    pcs = PC[:, cidx * DS:(cidx + 1) * DS]
                    t.matmul(pD[0:DS, 128:128 + DS], SPTB[:], pcs,
                             start=True, stop=True)
                    t.matmul(pD[0:DS, 192:192 + DS], pcs, SPTB[:],
                             start=True, stop=True)
                    t.matmul(pC[0:DS, 0:512], pcs, SCTB[:, 0:512],
                             start=True, stop=False)
                    t.matmul(pC[0:DS, 0:512], I64H[:], CTC[:, 0:512],
                             start=False, stop=True)
                    t.matmul(pC[0:DS, 512:1024], pcs, SCTB[:, 512:1024],
                             start=True, stop=False)
                    tick(t.matmul(pC[0:DS, 512:1024], I64H[:], CTC[:, 512:1024],
                                  start=False, stop=True), "t")
                st[("t_span", j)] = cnt["t"]
                yield
                @blk.scalar
                def _(s, j=j):
                    s.wait_ge(s_t, st[("t_span", j)])
                    s.activation(SPB[:], pD[0:DS, 128:128 + DS], AF.Copy,
                                 bias=0.0, scale=1.0)
                    s.activation(SPTB[:], pD[0:DS, 192:192 + DS], AF.Copy,
                                 bias=0.0, scale=1.0)
                    s.activation(SCTB[:, 0:512], pC[0:DS, 0:512], AF.Copy,
                                 bias=0.0, scale=1.0)
                    tick(s.activation(SCTB[:, 512:1024], pC[0:DS, 512:1024],
                                      AF.Copy, bias=0.0, scale=1.0), "s")
                span_s[0] = cnt["s"]
                if j < 3:
                    yield
            yield
            @blk.gpsimd
            def _(g):
                g.wait_ge(s_v, st[("v_y", 3)])
                g.wait_ge(s_sp, cnt["sp"])
                for kd in range(KD):
                    tick(g.dma_start(out=sprp_d[kd, :, tt * TT:(tt + 1) * TT],
                                     in_=RP[:, kd * TT:(kd + 1) * TT]), "sp", 16)
            rp_tile_spill[tt] = cnt["sp"]
            rp_prev[0] = cnt["sp"]


        pending = []
        def pump():
            if pending:
                try:
                    next(pending[0])
                except StopIteration:
                    pending.pop(0)
        def pump_all():
            while pending:
                try:
                    next(pending[0])
                except StopIteration:
                    pending.pop(0)

        def emit_xload(tt, pair_ticks, init_waits=None):
            marks = []
            @blk.sync
            def _(sy):
                if init_waits:
                    for k, val in init_waits:
                        sy.wait_ge(SEMS[k], val)
                for kp in range(KD // 2):
                    if pair_ticks is not None:
                        sy.wait_ge(s_v, pair_ticks[kp])
                    xk = "x%d" % kp
                    for i in (0, 1):
                        kd = 2 * kp + i
                        tick(sy.dma_start(out=XF[:, kd * TT:(kd + 1) * TT],
                                          in_=x_d[kd, :, tt * TT:(tt + 1) * TT]),
                             xk, 16)
                    marks.append((xk, cnt[xk]))
            return marks

        LNST = {}
        XMARK = {}
        xb0 = emit_xb_dma(0, [("wg", wg_done)])
        LNST[0] = emit_ln_stats([("wg", xb0), ("v", init_v)], XB, 0)
        XMARK[0] = emit_xload(0, None, init_waits=[("w", w_done), ("v", init_v)])

        for tt in range(NTILE):
            st1 = LNST[tt]
            xw = [[m] for m in XMARK[tt]]
            a0 = emit_ln_tail("q", 0, st1, XF, pair_waits=xw)
            a1 = emit_ln_tail("q", 1, st1, XF, pair_waits=xw)
            if tt + 1 < NTILE:
                xb_n = emit_xb_dma(tt + 1, [("t", xb_guard[0])])
                def hook(tn=tt + 1, xbn=xb_n):
                    LNST[tn] = emit_ln_stats_a([("wg", xbn)], XB, 0)
                def hook2(tn=tt + 1):
                    emit_ln_stats_b(LNST[tn])
            else:
                hook = hook2 = None
            ffn1_v = emit_ffn(tt, a1, XF, True, pump=pump, mid_hook=hook,
                              mid_hook2=hook2)
            pump_all()
            st2 = emit_ln_stats([("v", ffn1_v), ("t", xb_guard[0])], XB, 1,
                                copy_src=XF)
            b0 = emit_ln_tail("ln2", 0, st2, XF, apply_waits=[("sp", rp_prev[0])])
            pt = [0] * (KD // 2)
            b1 = emit_ln_tail("ln2", 1, st2, XF, apply_waits=[("sp", rp_prev[0])],
                              pair_ticks=pt)
            pending.append(quad_stages_p1(tt, b0, b1))
            pump(); pump()
            if tt + 1 < NTILE:
                XMARK[tt + 1] = emit_xload(tt + 1, pt)
            if tt == NTILE - 1:
                pump_all()

        @blk.gpsimd
        def _(g):
            g.wait_ge(s_s, span_s[0])
            g.wait_ge(s_sp, cnt["sp"])
            tick(g.dma_start(out=ccin_d[:, 0:DS], in_=SPB[:]), "sp", 16)
            tick(g.dma_start(out=ccin_d[:, DS:DS + D], in_=SCTB[:]), "sp", 16)
        cc_staged = cnt["sp"]

        @blk.gpsimd
        def _(g):
            g.wait_ge(s_sp, cc_staged)
            if sim1:
                for p in range(W):
                    g.dma_start(out=ccout_d[p * DS:(p + 1) * DS, :],
                                in_=ccin_d[:]).then_inc(s_cc, 16)
            else:
                g.collective_compute(
                    "AllGather", OP.bypass,
                    replica_groups=[list(range(W))],
                    ins=[ccin_d.ap().opt()],
                    outs=[ccout_d.ap().opt()],
                ).then_inc(s_cc, 1)
            g.wait_ge(s_cc, 16 * W if sim1 else 1)
            tick(g.dma_start(out=GB[:], in_=ccout_d[0:DS, :]), "g", 16)
        g_gathered = cnt["g"]

        @blk.vector
        def _(v):
            v.wait_ge(s_g, g_gathered)
            tick(v.memset(ZSB[:], 0.0), "v")
        z_prev = ("v", cnt["v"])
        for p in range(W):
            @blk.vector
            def _(v):
                v.wait_ge(s_g, g_gathered + p * 16)
                v.wait_ge(SEMS[z_prev[0]], z_prev[1])
                v.scalar_tensor_tensor(PBL[:], GB[:, 0:DS],
                                       LAM[:, p:p + 1], LAMI[:, p * DS:(p + 1) * DS],
                                       OP.mult, OP.add)
                tick(v.tensor_scalar(CBL[:], GB[:, DS:DS + D],
                                     LAM[:, p:p + 1], None, OP.mult), "v")
            v_bl = cnt["v"]
            if p + 1 < W:
                @blk.gpsimd
                def _(g):
                    g.wait_ge(s_v, v_bl)
                    tick(g.dma_start(out=GB[:],
                                     in_=ccout_d[(p + 1) * DS:(p + 2) * DS, :]), "g", 16)
            @blk.tensor
            def _(t):
                t.wait_ge(s_v, v_bl)
                t.matmul(pC[0:DS, 0:512], PBL[:], ZSB[:, 0:512], start=True, stop=False)
                t.matmul(pC[0:DS, 0:512], I64H[:], CBL[:, 0:512], start=False, stop=True)
                t.matmul(pC[0:DS, 512:1024], PBL[:], ZSB[:, 512:1024], start=True, stop=False)
                tick(t.matmul(pC[0:DS, 512:1024], I64H[:], CBL[:, 512:1024],
                              start=False, stop=True), "t")
            t_z = cnt["t"]
            @blk.scalar
            def _(s):
                s.wait_ge(s_t, t_z)
                s.activation(ZSB[:, 0:512], pC[0:DS, 0:512], AF.Copy, bias=0.0, scale=1.0)
                tick(s.activation(ZSB[:, 512:1024], pC[0:DS, 512:1024], AF.Copy,
                                  bias=0.0, scale=1.0), "s")
            z_prev = ("s", cnt["s"])

        def zslot(tt, ch):
            if ch == 0:
                return ZSB if tt % 2 == 0 else ZC1
            return ZH[:, (ch - 1) * D: ch * D]

        ct_consumed = [0, 0]
        zstate = {"prev": z_prev}

        def chain_stages_p2(tt, ch):
            cidx = tt * NCH + ch
            slot = (cidx % 2) * D
            rckey = "rc0" if cidx % 2 == 0 else "rc1"
            src = zslot(tt, ch)
            dst = zslot(tt, ch + 1) if ch + 1 < NCH else zslot(tt + 1, 0)
            st = {}
            zp = zstate["prev"]
            @blk.sync
            def _(sy):
                sy.wait_ge(s_t, ct_consumed[cidx % 2])
                if tt == 0 and ch == 0:
                    sy.wait_ge(SEMS[zp[0]], zp[1])
                tick(sy.dma_start(out=CTL[:, slot:slot + D], in_=spct_d[cidx]), rckey, 16)
            r_ct = (rckey, cnt[rckey])
            yield
            @blk.tensor
            def _(t):
                t.wait_ge(SEMS[zp[0]], zp[1])
                t.wait_ge(SEMS[r_ct[0]], r_ct[1])
                pcs = PC[:, cidx * DS:(cidx + 1) * DS]
                t.matmul(pC[0:DS, 0:512], pcs, src[:, 0:512], start=True, stop=False)
                t.matmul(pC[0:DS, 0:512], I64H[:], CTL[:, slot:slot + 512],
                         start=False, stop=True)
                t.matmul(pC[0:DS, 512:1024], pcs, src[:, 512:1024], start=True, stop=False)
                tick(t.matmul(pC[0:DS, 512:1024], I64H[:], CTL[:, slot + 512:slot + D],
                              start=False, stop=True), "t")
            ct_consumed[cidx % 2] = cnt["t"]
            st["t_zc"] = cnt["t"]
            yield
            @blk.scalar
            def _(s):
                s.wait_ge(s_t, st["t_zc"])
                s.activation(dst[:, 0:512], pC[0:DS, 0:512], AF.Copy, bias=0.0, scale=1.0)
                tick(s.activation(dst[:, 512:1024], pC[0:DS, 512:1024], AF.Copy,
                                  bias=0.0, scale=1.0), "s")
            zstate["prev"] = ("s", cnt["s"])


        def emit_ycorr(tt, r_loaded, prev_ffn_v):
            marks = {}
            t_pd_free = cnt["t"]
            for ch in range(NCH):
                csl = [slice(kd * TT + ch * CH, kd * TT + (ch + 1) * CH)
                       for kd in range(KD)]
                src = zslot(tt, ch)
                for wave in range(2):
                    @blk.tensor
                    def _(t):
                        if ch == 0 and wave == 0:
                            t.wait_ge(s_r, r_loaded)
                            t.wait_ge(SEMS[zstate["prev"][0]], zstate["prev"][1])
                            t.wait_ge(s_v, cnt["v"])
                        else:
                            t.wait_ge(s_v, marks["x2"])
                        for j in range(4):
                            kd = wave * 4 + j
                            i = t.matmul(pD[:, j * 128:(j + 1) * 128],
                                         src[:, kd * 128:(kd + 1) * 128],
                                         QHL[:, ch * CH:(ch + 1) * CH],
                                         start=True, stop=True)
                        tick(i, "t")
                    t_y = cnt["t"]
                    @blk.vector
                    def _(v):
                        v.wait_ge(s_t, t_y)
                        if ch == 0 and wave == 0:
                            v.wait_ge(s_v, prev_ffn_v)
                            v.wait_ge(s_sp, rp_tile_spill[NTILE - 1])
                        for j in range(4):
                            kd = wave * 4 + j
                            i = v.tensor_tensor(RP[:, csl[kd]], HA[:, csl[kd]],
                                                pD[:, j * 128:(j + 1) * 128], OP.add)
                        tick(i, "v")
                    marks["x2"] = cnt["v"]
            return marks["x2"]

        prev_ffn2_v = [0]
        prev_out = [0]
        p2_reload = [0] * NTILE

        def emit_reload(tt):
            @blk.sync
            def _(sy):
                sy.wait_ge(s_sp, rp_tile_spill[tt])
                sy.wait_ge(s_r, cnt["r"])
                if tt > 0:
                    sy.wait_ge(s_v, p2_reload_guard[0])
                else:
                    sy.wait_ge(s_t, cnt["t"])
                    sy.wait_ge(s_v, cnt["v"])
                for kd in range(KD):
                    tick(sy.dma_start(out=HA[:, kd * TT:(kd + 1) * TT],
                                      in_=sprp_d[kd, :, tt * TT:(tt + 1) * TT]), "r", 16)
                for ch in range(NCH):
                    tick(sy.dma_start(out=QHL[:, ch * CH:(ch + 1) * CH],
                                      in_=spqh_d[tt * NCH + ch]), "r", 16)
            p2_reload[tt] = cnt["r"]

        p2_reload_guard = [0]
        emit_reload(0)
        for ch in range(NCH):
            pending.append(chain_stages_p2(0, ch))
        pump_all()

        for tt in range(NTILE):
            x2_v = emit_ycorr(tt, p2_reload[tt], prev_ffn2_v[0])
            p2_reload_guard[0] = x2_v
            st3 = emit_ln_stats([("v", x2_v)], RP, 0, pump=pump)
            c0 = emit_ln_tail("q", 0, st3, RP)
            if tt + 1 < NTILE:
                emit_reload(tt + 1)
                for ch in range(NCH):
                    pending.append(chain_stages_p2(tt + 1, ch))
            c1 = emit_ln_tail("q", 1, st3, RP)
            ffn2_v = emit_ffn(tt, c1, RP, False,
                              evac_waits=[("o", prev_out[0])], pump=pump)
            pump_all()
            prev_ffn2_v[0] = ffn2_v


            @blk.sync
            def _(sy):
                sy.wait_ge(s_v, ffn2_v)
                for kd in range(KD):
                    tick(sy.dma_start(out=out_d[kd, :, tt * TT:(tt + 1) * TT],
                                      in_=XF[:, kd * TT:(kd + 1) * TT]), "o", 16)
            prev_out[0] = cnt["o"]

        @blk.sync
        def _(sy):
            sy.wait_ge(s_o, cnt["o"])
    return nc


def threefry2x32_np(k0, k1, c0, c1):
    ks0, ks1 = np.uint32(k0), np.uint32(k1)
    ks2 = np.uint32(ks0 ^ ks1 ^ np.uint32(0x1BD11BDA))
    rot = [[13, 15, 26, 6], [17, 29, 16, 24]]
    def rotl(x, d):
        return ((x << np.uint32(d)) | (x >> np.uint32(32 - d))).astype(np.uint32)
    x0 = (c0 + ks0).astype(np.uint32); x1 = (c1 + ks1).astype(np.uint32)
    ks = [ks0, ks1, ks2]
    for i in range(5):
        r = rot[0] if i % 2 == 0 else rot[1]
        for j in range(4):
            x0 = (x0 + x1).astype(np.uint32)
            x1 = rotl(x1, r[j])
            x1 = (x1 ^ x0).astype(np.uint32)
        x0 = (x0 + ks[(i + 1) % 3]).astype(np.uint32)
        x1 = (x1 + ks[(i + 2) % 3] + np.uint32(i + 1)).astype(np.uint32)
    return x0, x1


def host_prep(inp):
    x = np.ascontiguousarray(np.asarray(inp["L0_attn_x"], np.float32))
    B, S, Dm = x.shape
    xf = x.reshape(B * S, Dm)
    n = B * S * Dm
    c2 = np.arange(n, dtype=np.uint32)
    c1 = np.zeros(n, np.uint32)
    a0, a1 = threefry2x32_np(0, 42, c1, c2)
    bits = (a0 ^ a1).astype(np.uint32)
    f = ((bits >> np.uint32(9)) | np.uint32(0x3F800000)).view(np.float32) - np.float32(1.0)
    keep = np.maximum(np.float32(0.0), f) < np.float32(0.8)
    mask = np.where(keep.reshape(B * S, Dm), np.float16(0.625), np.float16(0.0))

    w1 = np.asarray(inp["L0_ffn_w1"], np.float32)
    w2 = np.asarray(inp["L0_ffn_w2"], np.float32)
    wv = np.asarray(inp["L0_attn_Wv"], np.float32)
    wk = np.asarray(inp["L0_attn_Wk"], np.float32)
    wq = np.asarray(inp["L0_attn_Wq"], np.float32)
    w1t = np.ascontiguousarray(w1.T.reshape(KD, 128, Hh))
    w2t = np.ascontiguousarray(w2.T.reshape(KH, 128, D))
    wvt = np.ascontiguousarray(wv.T.reshape(KD, 128, D))
    wkt = np.ascontiguousarray(wk.T.reshape(KD, 128, DS))
    wqt = np.ascontiguousarray(wq.T.reshape(KD, 128, DS))
    bias_h = np.ascontiguousarray(np.asarray(inp["L0_ffn_b1"], np.float32)
                                  .reshape(KH, 128).T)
    vecd = np.zeros((128, 24), np.float32)
    vecd[:, 0:8] = np.asarray(inp["L0_ffn_b2"], np.float32).reshape(KD, 128).T
    vecd[:, 8:16] = np.asarray(inp["L0_attn_gamma"], np.float32).reshape(KD, 128).T
    vecd[:, 16:24] = np.asarray(inp["L0_attn_beta"], np.float32).reshape(KD, 128).T
    bias_e = np.stack([np.asarray(inp["L0_attn_bk"], np.float32),
                       np.asarray(inp["L0_attn_bq"], np.float32)], axis=1)
    bvrow = np.asarray(inp["L0_attn_bv"], np.float32).reshape(1, D)
    bkrow = np.asarray(inp["L0_attn_bk"], np.float32).reshape(1, DS)
    consts = np.zeros((128, 576), np.float32)
    consts[:, 0:128] = np.tril(np.ones((128, 128), np.float32), -1)
    consts[:, 128:256] = np.triu(np.ones((128, 128), np.float32), 1)
    consts[:, 256:384] = 2.0 * np.eye(128, dtype=np.float32)
    consts[:, 384:512] = np.eye(128, dtype=np.float32)
    consts[0:DS, 512:576] = np.eye(DS, dtype=np.float32)
    i64h = np.eye(DS, dtype=np.float16)

    shared = dict(w1t=w1t, w2t=w2t, wvt=wvt, wkt=wkt, wqt=wqt, bias_h=bias_h,
                  vecd=vecd, bias_e=bias_e, bvrow=bvrow, bkrow=bkrow,
                  consts=consts, i64h=i64h)
    in_maps = []
    spanlen = B * S // W
    for c in range(W):
        sl = slice(c * spanlen, (c + 1) * spanlen)
        xs = np.ascontiguousarray(xf[sl].T.reshape(KD, 128, SPAN))
        ms = np.ascontiguousarray(mask[sl].T.reshape(KD, 128, SPAN))
        lam = np.zeros((DS, W), np.float32)
        g0 = 4 * (c // 4)
        for p in range(g0, c):
            lam[:, p] = 1.0
        lamI = np.zeros((W, DS, DS), np.float16)
        for p in range(W):
            lamI[p] = np.eye(DS, dtype=np.float16) * (0.0 if g0 <= p < c else 1.0)
        lamI = np.ascontiguousarray(lamI.transpose(1, 0, 2).reshape(DS, W * DS))
        m = dict(shared)
        m["x"] = xs
        m["mask"] = ms
        m["lam"] = lam
        m["lamI"] = lamI
        in_maps.append(m)
    return in_maps


def gather_out(results):
    outs = []
    for c in range(W):
        o = results[c]["out"]
        outs.append(np.ascontiguousarray(o.reshape(D, SPAN).T))
    return np.concatenate(outs, axis=0)


def kernel(**inputs):
    import concourse.bass_utils as bass_utils
    nc = build_program("full")
    in_maps = host_prep(inputs)
    res = bass_utils.run_bass_kernel_spmd(nc, in_maps, core_ids=list(range(W)))
    out = gather_out(res.results)
    return out.reshape(2, 16384, 1024)


# version 13
# speedup vs baseline: 1.3032x; 1.3032x over previous
import dataclasses
import numpy as np
from contextlib import ExitStack
import concourse.bass as bass
import concourse.mybir as mybir

MMPM = mybir.MatmulPerfMode


def drap(sl2d, pair_stride, inner_n):
    return dataclasses.replace(sl2d, ap=type(sl2d.ap)(
        [list(sl2d.ap[0]), [pair_stride, 2], [1, inner_n]]))

F32 = mybir.dt.float32
F16 = mybir.dt.float16
F8 = mybir.dt.float8e4
AF = mybir.ActivationFunctionType
OP = mybir.AluOpType

D = 1024
Hh = 4096
DS = 64
SPAN = 4096
TT = 512
NTILE = SPAN // TT
CH = 128
NCH = TT // CH
KD = D // 128
KH = Hh // 128
LR = float(np.float32(0.01))
INV_D = float(np.float32(1.0) / np.float32(1024.0))
EPS = float(np.float32(1e-5))
W = 8
HTT = TT // 2


def build_program(stage="full", sim1=False):
    nc = bass.Bass("TRN2", target_bir_lowering=False)
    x_d = nc.dram_tensor("x", [KD, 128, SPAN], F32, kind="ExternalInput")
    mk_d = nc.dram_tensor("mask", [KD, 128, SPAN], F16, kind="ExternalInput")
    w1_d = nc.dram_tensor("w1t", [KD, 128, Hh], F32, kind="ExternalInput")
    w2_d = nc.dram_tensor("w2t", [KH, 128, D], F32, kind="ExternalInput")
    wv_d = nc.dram_tensor("wvt", [KD, 128, D], F32, kind="ExternalInput")
    wk_d = nc.dram_tensor("wkt", [KD, 128, DS], F32, kind="ExternalInput")
    wq_d = nc.dram_tensor("wqt", [KD, 128, DS], F32, kind="ExternalInput")
    bh_d = nc.dram_tensor("bias_h", [128, KH], F32, kind="ExternalInput")
    vd_d = nc.dram_tensor("vecd", [128, 24], F32, kind="ExternalInput")
    be_d = nc.dram_tensor("bias_e", [DS, 2], F32, kind="ExternalInput")
    bv_d = nc.dram_tensor("bvrow", [1, D], F32, kind="ExternalInput")
    bk_d = nc.dram_tensor("bkrow", [1, DS], F32, kind="ExternalInput")
    cst_d = nc.dram_tensor("consts", [128, 576], F32, kind="ExternalInput")
    i64_d = nc.dram_tensor("i64h", [DS, DS], F16, kind="ExternalInput")
    lam_d = nc.dram_tensor("lam", [DS, W], F32, kind="ExternalInput")
    lamI_d = nc.dram_tensor("lamI", [DS, W * DS], F16, kind="ExternalInput")
    out_d = nc.dram_tensor("out", [KD, 128, SPAN], F32, kind="ExternalOutput")

    sprp_d = nc.dram_tensor("sp_rp", [KD, 128, SPAN], F16)
    spqh_d = nc.dram_tensor("sp_qh", [NTILE * NCH, DS, CH], F16)
    spct_d = nc.dram_tensor("sp_ct", [NTILE * NCH, DS, D], F16)
    ccin_d = nc.dram_tensor("cc_in", [DS, DS + D], F16)
    ccout_d = nc.dram_tensor("cc_out", [W * DS, DS + D], F16)

    es = ExitStack()
    def sb(name, shape, dt):
        return es.enter_context(nc.sbuf_tensor(name, shape, dt))
    def psum(name, shape, dt=F32):
        return es.enter_context(nc.psum_tensor(name, shape, dt))
    def sem(name):
        return es.enter_context(nc.semaphore(name))

    with es:
        W1Q = sb("W1Q", [128, KD * Hh], F8)
        W2Q = sb("W2Q", [128, KH * D], F8)
        WVT = sb("WVT", [128, KD * D], F16)
        WKT = sb("WKT", [128, KD * DS], F16)
        WQT = sb("WQT", [128, KD * DS], F16)
        B1Q = sb("B1Q", [128, KH], F32)
        B8 = sb("B8", [128, KH], F8)
        B2Q = sb("B2Q", [128, KD], F32)
        B28 = sb("B28", [128, KD], F8)
        GAM = sb("GAM", [128, KD], F32)
        BET = sb("BET", [128, KD], F32)
        BKE = sb("BKE", [DS, 2], F32)
        BVR = sb("BVR", [1, D], F16)
        BKR = sb("BKR", [1, DS], F16)
        ML = sb("ML", [128, 128], F16)
        MU = sb("MU", [128, 128], F16)
        TWOI = sb("TWOI", [128, 128], F16)
        I128F = sb("I128F", [128, 128], F16)
        I64F = sb("I64F", [DS, DS], F32)
        I64H = sb("I64H", [DS, DS], F16)
        LAM = sb("LAM", [DS, W], F32)
        LAMI = sb("LAMI", [DS, W * DS], F16)
        ONE16 = sb("ONE16", [128, 1], F16)
        ONE1F = sb("ONE1F", [1, 128], F32)
        ONE1H = sb("ONE1H", [1, 128], F16)
        EPSA = sb("EPSA", [1, 1], F32)
        ROWS = sb("ROWS", [1, 4 * TT], F32)
        XF = sb("XF", [128, KD * TT], F32)
        XB = sb("XB", [128, KD * TT], F16)
        MKS = sb("MKS", [128, 2 * TT], F16)
        LQ = sb("LQ", [128, KD * TT], F8)
        H8 = sb("H8", [128, KH * TT], F8)
        H16T = sb("H16T", [128, 2 * TT], F16)
        Y16T = sb("Y16T", [128, 2 * TT], F16)
        F32T = sb("F32T", [128, TT], F32)
        TMPA = sb("TMPA", [128, 2 * CH], F16)
        TMPB = sb("TMPB", [128, 2 * CH], F16)
        TMPF = sb("TMPF", [128, 2 * HTT], F32)
        RBF = sb("RBF", [128, 2 * HTT], F32)
        RB16 = sb("RB16", [128, 2], F16)
        HA = sb("HA", [128, KD * TT], F16)
        RP = sb("RP", [128, KD * TT], F16)
        VSB = sb("VSB", [128, 2 * D], F16)
        KSB = sb("KSB", [DS, 4 * CH], F16)
        QSB = sb("QSB", [DS, 4 * CH], F16)
        KTM = sb("KTM", [128, 4 * DS], F16)
        TB = sb("TB", [128, 4 * 128], F16)
        MSB = sb("MSB", [128, 4 * 128], F16)
        MTSB = sb("MTSB", [128, 4 * 128], F16)
        HB = sb("HB", [128, 4 * 128], F16)
        WTT = sb("WTT", [128, 2 * 128], F16)
        UV = sb("UV", [128, 2 * D], F16)
        UK = sb("UK", [128, 2 * DS], F16)
        QH = sb("QH", [DS, CH], F16)
        PC = sb("PC", [DS, NTILE * NCH * DS], F16)
        CTC = sb("CTC", [DS, D], F16)
        SPB = sb("SPB", [DS, DS], F16)
        SPTB = sb("SPTB", [DS, DS], F16)
        SCTB = sb("SCTB", [DS, D], F16)
        ZSB = sb("ZSB", [DS, D], F16)
        ZC1 = sb("ZC1", [DS, D], F16)
        ZH = sb("ZH", [DS, (NCH - 1) * D], F16)
        QHL = sb("QHL", [DS, NCH * CH], F16)
        CTL = sb("CTL", [DS, 2 * D], F16)
        GB = CTL[0:DS, 0:DS + D]
        PBL = QH[0:DS, 0:DS]
        CBL = CTC

        pA = [psum("pA0", [128, TT]), psum("pA1", [128, TT])]
        pB = [psum("pB0", [128, TT]), psum("pB1", [128, TT])]
        pC = psum("pC", [128, D])
        pD = psum("pD", [128, 512])
        pE = psum("pE", [128, 512])
        pE2A = pD[:, 384:512]

        s_w = sem("s_w"); s_wg = sem("s_wg"); s_x = sem("s_x")
        s_x0 = sem("s_x0"); s_x1 = sem("s_x1"); s_x2 = sem("s_x2"); s_x3 = sem("s_x3")
        s_sp = sem("s_sp"); s_r = sem("s_r")
        s_rc0 = sem("s_rc0"); s_rc1 = sem("s_rc1")
        s_mk0 = sem("s_mk0"); s_mk1 = sem("s_mk1")
        s_mk2 = sem("s_mk2"); s_mk3 = sem("s_mk3")
        s_o = sem("s_o"); s_cc = sem("s_cc"); s_g2 = sem("s_g2")
        s_t = sem("s_t"); s_v = sem("s_v"); s_s = sem("s_s"); s_g = sem("s_g")

        cnt = {"t": 0, "v": 0, "s": 0, "g": 0, "w": 0, "wg": 0, "x": 0, "sp": 0,
               "r": 0, "o": 0, "rc0": 0, "rc1": 0, "mk0": 0, "mk1": 0,
               "mk2": 0, "mk3": 0, "x0": 0, "x1": 0, "x2": 0, "x3": 0, "g2": 0}
        SEMS = {"t": s_t, "v": s_v, "s": s_s, "g": s_g, "w": s_w, "wg": s_wg,
                "x": s_x, "sp": s_sp, "r": s_r, "o": s_o, "rc0": s_rc0, "rc1": s_rc1,
                "mk0": s_mk0, "mk1": s_mk1, "mk2": s_mk2, "mk3": s_mk3,
                "x0": s_x0, "x1": s_x1, "x2": s_x2, "x3": s_x3, "g2": s_g2}

        def tick(instr, key, n=1):
            cnt[key] += n
            instr.then_inc(SEMS[key], n)
            return cnt[key]

        blk = es.enter_context(nc.Block())

        @blk.gpsimd
        def _(g):
            for kd in range(KD):
                tick(g.dma_start(out=W1Q[:, kd * Hh:(kd + 1) * Hh], in_=w1_d[kd]), "wg", 16)
            for kh in range(KH):
                if kh == 16:
                    g.wait_ge(s_wg, cnt["wg"])
                tick(g.dma_start(out=W2Q[:, kh * D:(kh + 1) * D], in_=w2_d[kh]), "wg", 16)
            g.wait_ge(s_wg, cnt["wg"])
            for kd in range(KD):
                tick(g.dma_start(out=WVT[:, kd * D:(kd + 1) * D], in_=wv_d[kd]), "wg", 16)
            for kd in range(KD):
                tick(g.dma_start(out=WKT[:, kd * DS:(kd + 1) * DS], in_=wk_d[kd]), "wg", 16)
                tick(g.dma_start(out=WQT[:, kd * DS:(kd + 1) * DS], in_=wq_d[kd]), "wg", 16)
            tick(g.dma_start(out=B8[:], in_=bh_d[:]), "wg", 16)
            tick(g.dma_start(out=B28[:], in_=vd_d[:, 0:KD]), "wg", 16)
            tick(g.dma_start(out=BVR[:], in_=bv_d[:]), "wg", 16)
            tick(g.dma_start(out=BKR[:], in_=bk_d[:]), "wg", 16)
            tick(g.dma_start(out=LAMI[:], in_=lamI_d[:]), "wg", 16)
            tick(g.dma_start(out=ML[:], in_=cst_d[:, 0:128]), "wg", 16)
            tick(g.dma_start(out=MU[:], in_=cst_d[:, 128:256]), "wg", 16)
            tick(g.dma_start(out=TWOI[:], in_=cst_d[:, 256:384]), "wg", 16)
            tick(g.dma_start(out=I128F[:], in_=cst_d[:, 384:512]), "wg", 16)

        @blk.sync
        def _(sy):
            tick(sy.dma_start(out=GAM[:], in_=vd_d[:, 8:16]), "w", 16)
            tick(sy.dma_start(out=BET[:], in_=vd_d[:, 16:24]), "w", 16)
            tick(sy.dma_start(out=BKE[:], in_=be_d[:]), "w", 16)
            tick(sy.dma_start(out=I64F[:], in_=cst_d[0:DS, 512:576]), "w", 16)
            tick(sy.dma_start(out=I64H[:], in_=i64_d[:]), "w", 16)
            tick(sy.dma_start(out=LAM[:], in_=lam_d[:]), "w", 16)
        w_done = cnt["w"]
        wg_done = cnt["wg"]

        @blk.scalar
        def _(s):
            s.preload_act_table(AF.Sqrt)

        @blk.vector
        def _(v):
            v.wait_ge(s_w, w_done)
            v.wait_ge(s_wg, wg_done)
            v.memset(ONE16[:], 1.0)
            v.memset(RB16[:], 0.0)
            v.memset(ONE1F[:], 1.0)
            v.memset(ONE1H[:], 1.0)
            v.memset(EPSA[:], EPS)
            v.tensor_copy(B1Q[:], B8[:])
            v.tensor_copy(B2Q[:], B28[:])
            v.tensor_copy(SPB[:], I64H[:])
            v.tensor_copy(SPTB[:], I64H[:])
            tick(v.memset(SCTB[:], 0.0), "v")
        init_v = cnt["v"]

        rel = {"LQ0": ("t", 0), "LQ1": ("t", 0), "XB": ("t", 0),
               "H80": ("t", 0), "H81": ("t", 0)}

        pe_guard = {"v": 0, "s": 0}
        rbf_guard = [0]
        rows_guard = {0: ("t", 0), 1: ("t", 0)}
        xb_guard = [0]

        def rslot(p, which, hh=None, half=False):
            base = (2 * p + (1 if which == "C" else 0)) * TT
            if half:
                return ROWS[0:1, base + hh * HTT: base + (hh + 1) * HTT]
            return ROWS[0:1, base: base + TT]

        def emit_xb_dma(tt, xb_free):
            @blk.gpsimd
            def _(g):
                for k, val in xb_free:
                    g.wait_ge(SEMS[k], val)
                for kd in range(KD):
                    tick(g.dma_start(out=XB[:, kd * TT:(kd + 1) * TT],
                                     in_=x_d[kd, :, tt * TT:(tt + 1) * TT]), "wg", 16)
            return cnt["wg"]

        def emit_ln_stats_a(waits, src16, rpair, copy_src=None, pump=None):
            ST = {"rpair": rpair, "s_bc0": None, "v_ap0": None, "src16": src16}
            cw = [None] * (KD // 2)
            if copy_src is not None:
                for p in range(KD // 2):
                    @blk.scalar
                    def _(s, p=p):
                        if p == 0:
                            for k, val in waits:
                                s.wait_ge(SEMS[k], val)
                            s.wait_ge(s_t, xb_guard[0])
                        for kd in (2 * p, 2 * p + 1):
                            i = s.activation(XB[:, kd * TT:(kd + 1) * TT],
                                             copy_src[:, kd * TT:(kd + 1) * TT],
                                             AF.Copy, bias=0.0, scale=1.0)
                        tick(i, "s")
                    cw[p] = ("s", cnt["s"])
                if pump: pump()
            sm = [None] * (KD // 2)
            for p in range(KD // 2):
                @blk.tensor
                def _(t, p=p):
                    if p == 0:
                        for k, val in waits:
                            t.wait_ge(SEMS[k], val)
                        t.wait_ge(s_v, pe_guard["v"])
                        t.wait_ge(s_s, pe_guard["s"])
                    if cw[p] is not None:
                        t.wait_ge(SEMS[cw[p][0]], cw[p][1])
                    for kd in (2 * p, 2 * p + 1):
                        i = t.matmul(pE[0:1, 0:TT], ONE16[:],
                                     src16[:, kd * TT:(kd + 1) * TT],
                                     start=(kd == 0), stop=(kd == KD - 1))
                    tick(i, "t")
                sm[p] = cnt["t"]
            ST["t_sum"] = cnt["t"]
            if pump: pump()
            qm = [None] * (KD // 2)
            for p in range(KD // 2):
                @blk.scalar
                def _(s, p=p):
                    s.wait_ge(s_t, max(sm[p], xb_guard[0]))
                    for kd in (2 * p, 2 * p + 1):
                        i = s.activation(XB[:, kd * TT:(kd + 1) * TT],
                                         src16[:, kd * TT:(kd + 1) * TT],
                                         AF.Square, bias=0.0,
                                         scale=float(1.0 / 64.0))
                    tick(i, "s")
                qm[p] = cnt["s"]
            ST["s_sq"] = cnt["s"]
            ST["qm"] = qm
            return ST

        def emit_ln_stats_b(ST, pump=None):
            rpair = ST["rpair"]
            qm = ST["qm"]
            for p in range(KD // 2):
                @blk.tensor
                def _(t, p=p):
                    t.wait_ge(s_s, qm[p])
                    for kd in (2 * p, 2 * p + 1):
                        i = t.matmul(pE[32:33, 0:TT], ONE16[:],
                                     XB[:, kd * TT:(kd + 1) * TT],
                                     start=(kd == 0), stop=(kd == KD - 1))
                    tick(i, "t")
            ST["t_sq"] = cnt["t"]
            xb_guard[0] = cnt["t"]
            if pump: pump()
            B = rslot(rpair, "B")
            C = rslot(rpair, "C")
            rg = rows_guard[rpair]
            @blk.vector
            def _(v):
                v.wait_ge(s_t, ST["t_sq"])
                v.wait_ge(SEMS[rg[0]], rg[1])
                tick(v.tensor_scalar(B, pE[0:1, 0:TT], INV_D, None, OP.mult), "v")
                v.wait_ge(s_v, cnt["v"])
                tick(v.tensor_tensor(C, B, B, OP.mult), "v")
                v.wait_ge(s_v, cnt["v"])
                tick(v.scalar_tensor_tensor(B, pE[32:33, 0:TT], 4.0, C,
                                            OP.mult, OP.subtract), "v")
            v_var = cnt["v"]
            @blk.scalar
            def _(s):
                s.wait_ge(s_v, v_var)
                tick(s.activation(B, B, AF.Sqrt, bias=EPSA[:, 0:1]), "s")
            s_sd = cnt["s"]
            @blk.vector
            def _(v):
                v.wait_ge(s_s, s_sd)
                tick(v.reciprocal(C, B), "v")
                v.wait_ge(s_v, cnt["v"])
                tick(v.scalar_tensor_tensor(B, pE[0:1, 0:TT], INV_D, C,
                                            OP.mult, OP.mult), "v")
            ST["v_rows"] = cnt["v"]
            pe_guard["v"] = cnt["v"]
            if pump: pump()
            return ST

        def emit_ln_stats(waits, src16, rpair, copy_src=None, pump=None):
            ST = emit_ln_stats_a(waits, src16, rpair, copy_src, pump)
            return emit_ln_stats_b(ST, pump)

        def emit_ln_tail(out_kind, hh, ST, src, apply_waits=(), pair_waits=None,
                         pair_ticks=None):
            lq_rel = rel["LQ%d" % hh]
            rp = ST["rpair"]
            rbg = rbf_guard[0]
            @blk.tensor
            def _(t):
                if hh == 0:
                    t.wait_ge(s_v, ST["v_rows"])
                    t.wait_ge(s_s, pe_guard["s"])
                else:
                    t.wait_ge(s_s, ST["s_bc0"])
                t.matmul(pE[:, 0:HTT], ONE1F[:], rslot(rp, "C", hh, True),
                         start=True, stop=True)
                tick(t.matmul(pE[:, HTT:2 * HTT], ONE1F[:], rslot(rp, "B", hh, True),
                              start=True, stop=True), "t")
            t_bc = cnt["t"]
            rows_guard[rp] = ("t", t_bc)

            @blk.scalar
            def _(s):
                s.wait_ge(s_t, t_bc)
                if hh == 0:
                    s.wait_ge(s_v, rbg)
                else:
                    s.wait_ge(s_v, ST["v_ap0"])
                s.activation(RBF[:, 0:HTT], pE[:, 0:HTT], AF.Copy,
                             bias=0.0, scale=1.0)
                tick(s.activation(RBF[:, HTT:2 * HTT], pE[:, HTT:2 * HTT],
                                  AF.Copy, bias=0.0, scale=1.0), "s")
            s_bc = cnt["s"]
            pe_guard["s"] = s_bc
            if hh == 0:
                ST["s_bc0"] = s_bc

            @blk.vector
            def _(v):
                v.wait_ge(s_s, s_bc)
                if out_kind == "q":
                    v.wait_ge(SEMS[lq_rel[0]], lq_rel[1])
                for k, val in apply_waits:
                    v.wait_ge(SEMS[k], val)
                rstd = RBF[:, 0:HTT]
                mr = RBF[:, HTT:2 * HTT]
                tf = [TMPF[:, 0:HTT], TMPF[:, HTT:2 * HTT]]
                for kp in range(KD // 2):
                    kk = [2 * kp, 2 * kp + 1]
                    sls = [slice(kk[i] * TT + hh * HTT,
                                 kk[i] * TT + (hh + 1) * HTT) for i in (0, 1)]
                    if pair_waits is not None:
                        for k, val in pair_waits[kp]:
                            v.wait_ge(SEMS[k], val)
                    if kp > 0 or hh == 1:
                        v.wait_ge(s_v, cnt["v"])
                    v.tensor_tensor(tf[0], src[:, sls[0]], rstd, OP.mult)
                    tick(v.tensor_tensor(tf[1], src[:, sls[1]], rstd, OP.mult), "v")
                    v.wait_ge(s_v, cnt["v"])
                    if out_kind == "q":
                        v.tensor_tensor(LQ[:, sls[0]], tf[0], mr, OP.subtract)
                        tick(v.tensor_tensor(LQ[:, sls[1]], tf[1], mr,
                                             OP.subtract), "v")
                    else:
                        v.tensor_tensor(tf[0], tf[0], mr, OP.subtract)
                        tick(v.tensor_tensor(tf[1], tf[1], mr, OP.subtract), "v")
                        v.wait_ge(s_v, cnt["v"])
                        for i in (0, 1):
                            v.tensor_tensor(RP[:, sls[i]], src[:, sls[i]], tf[i],
                                            OP.add)
                        v.tensor_scalar(HA[:, sls[0]], tf[0],
                                        GAM[:, kk[0]:kk[0] + 1],
                                        BET[:, kk[0]:kk[0] + 1], OP.mult, OP.add)
                        tick(v.tensor_scalar(HA[:, sls[1]], tf[1],
                                             GAM[:, kk[1]:kk[1] + 1],
                                             BET[:, kk[1]:kk[1] + 1],
                                             OP.mult, OP.add), "v")
                    if pair_ticks is not None:
                        pair_ticks[kp] = cnt["v"]
            v_apply = cnt["v"]
            rbf_guard[0] = v_apply
            if hh == 0:
                ST["v_ap0"] = v_apply
            return v_apply

        mk_guard = {0: ("v", 0), 1: ("v", 0)}
        def emit_ffn(tt, ln_v, res_src, res_inplace, evac_waits=(), pump=None,
                     mid_hook=None, mid_hook2=None):
            s_preh = cnt["s"]
            h8_rel = rel["H80"]
            tm = {}; sm = {}; vm = {}
            for kh in range(KH):
                @blk.tensor
                def _(t):
                    if kh == 0:
                        t.wait_ge(s_v, ln_v)
                    if kh < 2:
                        t.wait_ge(s_s, s_preh)
                    else:
                        t.wait_ge(s_s, sm[kh - 2])
                    bank = pA[kh % 2]
                    for kp in range(KD // 2):
                        lw = drap(W1Q[:, 2 * kp * Hh + kh * 128:
                                      2 * kp * Hh + (kh + 1) * 128], Hh, 128)
                        rx = drap(LQ[:, 2 * kp * TT:(2 * kp + 1) * TT], TT, TT)
                        i = t.matmul(bank[:, 0:TT], lw, rx,
                                     start=(kp == 0), stop=(kp == KD // 2 - 1),
                                     perf_mode=MMPM.DoubleRow)
                    tick(i, "t")
                tm[kh] = cnt["t"]
                @blk.scalar
                def _(s):
                    s.wait_ge(s_t, tm[kh])
                    if kh >= 2:
                        s.wait_ge(s_v, vm[kh - 2])
                    tick(s.activation(H16T[:, (kh % 2) * TT:(kh % 2 + 1) * TT],
                                      pA[kh % 2][:, 0:TT], AF.Relu,
                                      bias=B1Q[:, kh:kh + 1], scale=1.0), "s")
                sm[kh] = cnt["s"]
                @blk.vector
                def _(v):
                    if kh == 0:
                        v.wait_ge(SEMS[h8_rel[0]], h8_rel[1])
                    v.wait_ge(s_s, sm[kh])
                    tick(v.tensor_copy(H8[:, kh * TT:(kh + 1) * TT],
                                       H16T[:, (kh % 2) * TT:(kh % 2 + 1) * TT]), "v")
                vm[kh] = cnt["v"]
                if pump:
                    pump()
                if kh == 20 and mid_hook:
                    mid_hook()
            rel["LQ0"] = rel["LQ1"] = ("t", tm[KH - 1])

            tm2 = {}; sm2 = {}; vm2 = {}
            def emit_add(v, pd):
                sl = slice(pd * TT, (pd + 1) * TT)
                pft = F32T[:, 0:TT]
                if res_inplace:
                    return v.tensor_tensor(XF[:, sl], XF[:, sl], pft, OP.add)
                return v.tensor_tensor(XF[:, sl], res_src[:, sl], pft, OP.add)
            for dv in range(KD):
                p2 = dv % 2
                mkkey = "mk%d" % p2
                g0 = mk_guard[p2]
                @blk.sync
                def _(sy):
                    sy.wait_ge(SEMS[g0[0]], g0[1])
                    tick(sy.dma_start(out=MKS[:, p2 * TT:(p2 + 1) * TT],
                                      in_=mk_d[dv, :, tt * TT:(tt + 1) * TT]), mkkey, 16)
                mk_ld = cnt[mkkey]
                @blk.tensor
                def _(t):
                    if dv == 0:
                        t.wait_ge(s_v, vm[KH - 1])
                    if dv < 2:
                        t.wait_ge(s_s, s_preh)
                    else:
                        t.wait_ge(s_s, sm2[dv - 2])
                    bank = pB[p2]
                    for kp in range(KH // 2):
                        lw = drap(W2Q[:, 2 * kp * D + dv * 128:
                                      2 * kp * D + (dv + 1) * 128], D, 128)
                        rx = drap(H8[:, 2 * kp * TT:(2 * kp + 1) * TT], TT, TT)
                        i = t.matmul(bank[:, 0:TT], lw, rx,
                                     start=(kp == 0), stop=(kp == KH // 2 - 1),
                                     perf_mode=MMPM.DoubleRow)
                    tick(i, "t")
                tm2[dv] = cnt["t"]
                @blk.scalar
                def _(s):
                    s.wait_ge(s_t, tm2[dv])
                    if dv >= 2:
                        s.wait_ge(s_v, vm2[dv - 2])
                    tick(s.activation(Y16T[:, p2 * TT:(p2 + 1) * TT],
                                      pB[p2][:, 0:TT], AF.Identity,
                                      bias=B2Q[:, dv:dv + 1], scale=1.0), "s")
                sm2[dv] = cnt["s"]
                @blk.vector
                def _(v):
                    if dv == 0:
                        for k, val in evac_waits:
                            v.wait_ge(SEMS[k], val)
                    v.wait_ge(s_s, sm2[dv])
                    v.wait_ge(SEMS[mkkey], mk_ld)
                    if dv > 0:
                        v.wait_ge(s_v, vm2[dv - 1])
                        tick(emit_add(v, dv - 1), "v")
                        v.wait_ge(s_v, cnt["v"])
                    tick(v.tensor_tensor(F32T[:, 0:TT],
                                         Y16T[:, p2 * TT:(p2 + 1) * TT],
                                         MKS[:, p2 * TT:(p2 + 1) * TT],
                                         OP.mult), "v")
                vm2[dv] = cnt["v"]
                mk_guard[p2] = ("v", cnt["v"])
                if pump:
                    pump()
                if dv == 1 and mid_hook2:
                    mid_hook2()
            @blk.vector
            def _(v):
                v.wait_ge(s_v, vm2[KD - 1])
                tick(emit_add(v, KD - 1), "v")
            rel["H80"] = rel["H81"] = ("t", tm2[KD - 1])
            return cnt["v"]

        qhct_prev = [0]
        rp_prev = [0]
        span_s = [0]
        rp_tile_spill = [0] * NTILE

        def quad_stages_p1(tt, ha0, ha1):
            st = {}
            csls = [[slice(kd * TT + ch * CH, kd * TT + (ch + 1) * CH)
                     for kd in range(KD)] for ch in range(NCH)]
            for g in (0, 1):
                @blk.tensor
                def _(t, g=g):
                    if g == 0:
                        t.wait_ge(s_v, ha0)
                        if span_s[0]:
                            t.wait_ge(s_s, span_s[0])
                    else:
                        t.wait_ge(s_v, ha1)
                        t.wait_ge(s_s, st[("s_kq", 0)])
                    for jj in (0, 1):
                        j = 2 * g + jj
                        for kd in range(KD):
                            t.matmul(pD[0:DS, jj * CH:(jj + 1) * CH],
                                     WKT[:, kd * DS:(kd + 1) * DS],
                                     HA[:, csls[j][kd]],
                                     start=(kd == 0), stop=(kd == KD - 1))
                        for kd in range(KD):
                            t.matmul(pD[0:DS, 256 + jj * CH:256 + (jj + 1) * CH],
                                     WQT[:, kd * DS:(kd + 1) * DS],
                                     HA[:, csls[j][kd]],
                                     start=(kd == 0), stop=(kd == KD - 1))
                        for kd in range(KD):
                            t.matmul(pC[0:CH, jj * DS:(jj + 1) * DS],
                                     HA[:, csls[j][kd]],
                                     WKT[:, kd * DS:(kd + 1) * DS],
                                     start=(kd == 0), stop=False)
                        i = t.matmul(pC[0:CH, jj * DS:(jj + 1) * DS], ONE1H[:],
                                     BKR[:], start=False, stop=True)
                    tick(i, "t")
                st[("t_kq", g)] = cnt["t"]
                yield
                @blk.scalar
                def _(s, g=g):
                    s.wait_ge(s_t, st[("t_kq", g)])
                    for jj in (0, 1):
                        j = 2 * g + jj
                        s.activation(KSB[:, j * CH:(j + 1) * CH],
                                     pD[0:DS, jj * CH:(jj + 1) * CH],
                                     AF.Identity, bias=BKE[:, 0:1], scale=1.0)
                        s.activation(QSB[:, j * CH:(j + 1) * CH],
                                     pD[0:DS, 256 + jj * CH:256 + (jj + 1) * CH],
                                     AF.Identity, bias=BKE[:, 1:2], scale=1.0)
                        i = s.activation(KTM[:, j * DS:(j + 1) * DS],
                                         pC[0:CH, jj * DS:(jj + 1) * DS],
                                         AF.Copy, bias=0.0, scale=1.0)
                    tick(i, "s")
                st[("s_kq", g)] = cnt["s"]
                yield
            @blk.tensor
            def _(t):
                t.wait_ge(s_s, st[("s_kq", 1)])
                for j in range(4):
                    i = t.matmul(pD[:, j * 128:(j + 1) * 128],
                                 KSB[:, j * CH:(j + 1) * CH],
                                 KSB[:, j * CH:(j + 1) * CH],
                                 start=True, stop=True)
                tick(i, "t")
            st["t_A"] = cnt["t"]
            yield
            for g in (0, 1):
                @blk.vector
                def _(v, g=g):
                    if g == 0:
                        v.wait_ge(s_t, st["t_A"])
                    else:
                        v.wait_ge(s_v, cnt["v"])
                    for jj in (0, 1):
                        j = 2 * g + jj
                        sl1 = slice(jj * 128, (jj + 1) * 128)
                        v.tensor_scalar(TMPA[:, sl1], pD[:, j * 128:(j + 1) * 128],
                                        LR, None, OP.mult)
                    tick(v.tensor_copy(RB16[:, 0:1], RB16[:, 0:1]), "v")
                    v.wait_ge(s_v, cnt["v"])
                    for jj in (0, 1):
                        sl1 = slice(jj * 128, (jj + 1) * 128)
                        v.tensor_tensor(TMPB[:, sl1], TMPA[:, sl1], ML[:], OP.mult)
                        v.tensor_tensor(TMPA[:, sl1], TMPA[:, sl1], MU[:], OP.mult)
                    tick(v.tensor_copy(RB16[:, 0:1], RB16[:, 0:1]), "v")
                    v.wait_ge(s_v, cnt["v"])
                    for jj in (0, 1):
                        j = 2 * g + jj
                        sl1 = slice(jj * 128, (jj + 1) * 128)
                        jsl = slice(j * 128, (j + 1) * 128)
                        v.tensor_tensor(TB[:, jsl], TMPB[:, sl1], I128F[:], OP.add)
                        v.tensor_tensor(MSB[:, jsl], I128F[:], TMPA[:, sl1],
                                        OP.subtract)
                        i = v.tensor_tensor(MTSB[:, jsl], I128F[:], TMPB[:, sl1],
                                            OP.subtract)
                    tick(i, "v")
                if g == 0:
                    yield
            prev = ("v", cnt["v"])
            yield
            for it in range(6):
                @blk.tensor
                def _(t):
                    t.wait_ge(SEMS[prev[0]], prev[1])
                    for j in range(4):
                        jsl = slice(j * 128, (j + 1) * 128)
                        i = t.matmul(pD[:, jsl], TB[:, jsl], MSB[:, jsl],
                                     start=True, stop=True)
                    tick(i, "t")
                st["t_G"] = cnt["t"]
                yield
                @blk.scalar
                def _(s):
                    s.wait_ge(s_t, st["t_G"])
                    for j in range(4):
                        jsl = slice(j * 128, (j + 1) * 128)
                        i = s.activation(HB[:, jsl], pD[:, jsl], AF.Copy,
                                         bias=0.0, scale=-1.0)
                    tick(i, "s")
                st["s_Gn"] = cnt["s"]
                @blk.tensor
                def _(t):
                    t.wait_ge(s_s, st["s_Gn"])
                    for j in range(4):
                        jsl = slice(j * 128, (j + 1) * 128)
                        j2 = slice(512 + j * 128, 512 + (j + 1) * 128)
                        t.matmul(pC[:, jsl], TWOI[:], MSB[:, jsl],
                                 start=True, stop=False)
                        t.matmul(pC[:, jsl], MTSB[:, jsl], HB[:, jsl],
                                 start=False, stop=True)
                        t.matmul(pC[:, j2], TWOI[:], MTSB[:, jsl],
                                 start=True, stop=False)
                        i = t.matmul(pC[:, j2], HB[:, jsl], MTSB[:, jsl],
                                     start=False, stop=True)
                    tick(i, "t")
                st["t_M"] = cnt["t"]
                yield
                @blk.scalar
                def _(s):
                    s.wait_ge(s_t, st["t_M"])
                    for j in range(4):
                        jsl = slice(j * 128, (j + 1) * 128)
                        j2 = slice(512 + j * 128, 512 + (j + 1) * 128)
                        s.activation(MSB[:, jsl], pC[:, jsl], AF.Copy,
                                     bias=0.0, scale=1.0)
                        i = s.activation(MTSB[:, jsl], pC[:, j2], AF.Copy,
                                         bias=0.0, scale=1.0)
                    tick(i, "s")
                prev = ("s", cnt["s"])
            yield
            for j in range(4):
                par = j % 2
                cidx = tt * NCH + j
                @blk.tensor
                def _(t, j=j, par=par):
                    t.wait_ge(SEMS[prev[0]], prev[1])
                    if j > 0:
                        t.wait_ge(s_s, span_s[0])
                    for half in range(2):
                        for kd in range(KD):
                            t.matmul(pC[0:CH, half * 512:(half + 1) * 512],
                                     HA[:, csls[j][kd]],
                                     WVT[:, kd * D + half * 512:
                                          kd * D + (half + 1) * 512],
                                     start=(kd == 0), stop=False)
                        i = t.matmul(pC[0:CH, half * 512:(half + 1) * 512],
                                     ONE1H[:], BVR[0:1, half * 512:(half + 1) * 512],
                                     start=False, stop=True)
                    tick(i, "t")
                st[("t_v", j)] = cnt["t"]
                yield
                @blk.scalar
                def _(s, j=j, par=par):
                    s.wait_ge(s_t, st[("t_v", j)])
                    if j >= 2:
                        s.wait_ge(s_t, st[("t_U", j - 2)])
                    s.activation(VSB[:, par * D:par * D + 512], pC[0:CH, 0:512],
                                 AF.Copy, bias=0.0, scale=1.0)
                    tick(s.activation(VSB[:, par * D + 512:par * D + 1024],
                                      pC[0:CH, 512:1024], AF.Copy,
                                      bias=0.0, scale=1.0), "s")
                st[("s_v", j)] = cnt["s"]
                yield
                @blk.tensor
                def _(t, j=j, par=par):
                    t.wait_ge(s_s, st[("s_v", j)])
                    m = MSB[:, j * 128:(j + 1) * 128]
                    t.matmul(pC[0:CH, 0:512], m, VSB[:, par * D:par * D + 512],
                             start=True, stop=True)
                    t.matmul(pC[0:CH, 512:1024], m,
                             VSB[:, par * D + 512:par * D + 1024],
                             start=True, stop=True)
                    t.matmul(pD[0:CH, 0:DS], m, KTM[:, j * DS:(j + 1) * DS],
                             start=True, stop=True)
                    tick(t.matmul(pD[:, 256 + par * 128:384 + par * 128],
                                  KSB[:, j * CH:(j + 1) * CH],
                                  QSB[:, j * CH:(j + 1) * CH],
                                  start=True, stop=True), "t")
                st[("t_U", j)] = cnt["t"]
                yield
                @blk.scalar
                def _(s, j=j, par=par):
                    s.wait_ge(s_t, st[("t_U", j)])
                    s.activation(UV[:, par * D:par * D + 512], pC[0:CH, 0:512],
                                 AF.Copy, bias=0.0, scale=LR)
                    s.activation(UV[:, par * D + 512:par * D + 1024],
                                 pC[0:CH, 512:1024], AF.Copy, bias=0.0, scale=LR)
                    tick(s.activation(UK[:, par * DS:(par + 1) * DS],
                                      pD[0:CH, 0:DS], AF.Copy, bias=0.0,
                                      scale=LR), "s")
                st[("s_U", j)] = cnt["s"]
                @blk.vector
                def _(v, j=j, par=par):
                    v.wait_ge(s_t, st[("t_U", j)])
                    tick(v.tensor_tensor(WTT[:, par * 128:(par + 1) * 128],
                                         pD[:, 256 + par * 128:384 + par * 128],
                                         MU[:], OP.mult), "v")
                st[("v_W", j)] = cnt["v"]
                yield
                for wave in range(2):
                    @blk.tensor
                    def _(t, j=j, par=par, wave=wave):
                        t.wait_ge(s_s, st[("s_U", j)])
                        t.wait_ge(s_v, st[("v_W", j)] if wave == 0
                                  else st[("v_y", j)])
                        for q in range(4):
                            kd = wave * 4 + q
                            i = t.matmul(pC[:, q * 128:(q + 1) * 128],
                                         UV[:, par * D + kd * 128:
                                             par * D + (kd + 1) * 128],
                                         WTT[:, par * 128:(par + 1) * 128],
                                         start=True, stop=True)
                        tick(i, "t")
                    st[("t_y", j)] = cnt["t"]
                    yield
                    @blk.vector
                    def _(v, j=j, wave=wave):
                        v.wait_ge(s_t, st[("t_y", j)])
                        v.wait_ge(s_sp, rp_prev[0])
                        for q in range(4):
                            kd = wave * 4 + q
                            i = v.tensor_tensor(RP[:, csls[j][kd]],
                                                RP[:, csls[j][kd]],
                                                pC[:, q * 128:(q + 1) * 128],
                                                OP.add)
                        tick(i, "v")
                    st[("v_y", j)] = cnt["v"]
                yield
                @blk.tensor
                def _(t, j=j, par=par):
                    t.wait_ge(s_v, st[("v_y", j)])
                    if span_s[0]:
                        t.wait_ge(s_s, span_s[0])
                    t.matmul(pD[0:DS, 0:128], UK[:, par * DS:(par + 1) * DS],
                             WTT[:, par * 128:(par + 1) * 128],
                             start=True, stop=True)
                    t.matmul(pD[0:DS, 128:128 + DS], UK[:, par * DS:(par + 1) * DS],
                             KTM[:, j * DS:(j + 1) * DS], start=True, stop=True)
                    t.matmul(pC[0:DS, 0:512], KTM[:, j * DS:(j + 1) * DS],
                             UV[:, par * D:par * D + 512], start=True, stop=True)
                    tick(t.matmul(pC[0:DS, 512:1024], KTM[:, j * DS:(j + 1) * DS],
                                  UV[:, par * D + 512:par * D + 1024],
                                  start=True, stop=True), "t")
                st[("t_QP", j)] = cnt["t"]
                yield
                @blk.vector
                def _(v, j=j, par=par, cidx=cidx):
                    v.wait_ge(s_t, st[("t_QP", j)])
                    v.wait_ge(s_sp, qhct_prev[0])
                    v.scalar_tensor_tensor(QH[:], pD[0:DS, 0:128], -1.0,
                                           QSB[:, j * CH:(j + 1) * CH],
                                           OP.mult, OP.add)
                    tick(v.scalar_tensor_tensor(PC[:, cidx * DS:(cidx + 1) * DS],
                                                pD[0:DS, 128:128 + DS], -1.0,
                                                I64F[:], OP.mult, OP.add), "v")
                st[("v_QP", j)] = cnt["v"]
                @blk.scalar
                def _(s, j=j):
                    s.wait_ge(s_t, st[("t_QP", j)])
                    s.wait_ge(s_sp, qhct_prev[0])
                    s.activation(CTC[:, 0:512], pC[0:DS, 0:512], AF.Copy,
                                 bias=0.0, scale=1.0)
                    tick(s.activation(CTC[:, 512:1024], pC[0:DS, 512:1024],
                                      AF.Copy, bias=0.0, scale=1.0), "s")
                st[("s_ctc", j)] = cnt["s"]
                yield
                @blk.gpsimd
                def _(g, j=j, cidx=cidx):
                    g.wait_ge(s_v, st[("v_QP", j)])
                    g.wait_ge(s_s, st[("s_ctc", j)])
                    g.wait_ge(s_sp, cnt["sp"])
                    tick(g.dma_start(out=spqh_d[cidx], in_=QH[:]), "sp", 16)
                    tick(g.dma_start(out=spct_d[cidx], in_=CTC[:]), "sp", 16)
                qhct_prev[0] = cnt["sp"]
                @blk.tensor
                def _(t, j=j, cidx=cidx):
                    t.wait_ge(s_v, st[("v_QP", j)])
                    t.wait_ge(s_s, st[("s_ctc", j)])
                    pcs = PC[:, cidx * DS:(cidx + 1) * DS]
                    t.matmul(pD[0:DS, 128:128 + DS], SPTB[:], pcs,
                             start=True, stop=True)
                    t.matmul(pD[0:DS, 192:192 + DS], pcs, SPTB[:],
                             start=True, stop=True)
                    t.matmul(pC[0:DS, 0:512], pcs, SCTB[:, 0:512],
                             start=True, stop=False)
                    t.matmul(pC[0:DS, 0:512], I64H[:], CTC[:, 0:512],
                             start=False, stop=True)
                    t.matmul(pC[0:DS, 512:1024], pcs, SCTB[:, 512:1024],
                             start=True, stop=False)
                    tick(t.matmul(pC[0:DS, 512:1024], I64H[:], CTC[:, 512:1024],
                                  start=False, stop=True), "t")
                st[("t_span", j)] = cnt["t"]
                yield
                @blk.scalar
                def _(s, j=j):
                    s.wait_ge(s_t, st[("t_span", j)])
                    s.activation(SPB[:], pD[0:DS, 128:128 + DS], AF.Copy,
                                 bias=0.0, scale=1.0)
                    s.activation(SPTB[:], pD[0:DS, 192:192 + DS], AF.Copy,
                                 bias=0.0, scale=1.0)
                    s.activation(SCTB[:, 0:512], pC[0:DS, 0:512], AF.Copy,
                                 bias=0.0, scale=1.0)
                    tick(s.activation(SCTB[:, 512:1024], pC[0:DS, 512:1024],
                                      AF.Copy, bias=0.0, scale=1.0), "s")
                span_s[0] = cnt["s"]
                if j < 3:
                    yield
            yield
            @blk.gpsimd
            def _(g):
                g.wait_ge(s_v, st[("v_y", 3)])
                g.wait_ge(s_sp, cnt["sp"])
                for kd in range(KD):
                    tick(g.dma_start(out=sprp_d[kd, :, tt * TT:(tt + 1) * TT],
                                     in_=RP[:, kd * TT:(kd + 1) * TT]), "sp", 16)
            rp_tile_spill[tt] = cnt["sp"]
            rp_prev[0] = cnt["sp"]


        pending = []
        def pump():
            if pending:
                try:
                    next(pending[0])
                except StopIteration:
                    pending.pop(0)
        def pump_n(n):
            for _ in range(n):
                pump()
        def pump_all():
            while pending:
                try:
                    next(pending[0])
                except StopIteration:
                    pending.pop(0)

        def emit_xload(tt, pair_ticks, init_waits=None):
            marks = []
            @blk.sync
            def _(sy):
                if init_waits:
                    for k, val in init_waits:
                        sy.wait_ge(SEMS[k], val)
                for kp in range(KD // 2):
                    if pair_ticks is not None:
                        sy.wait_ge(s_v, pair_ticks[kp])
                    xk = "x%d" % kp
                    for i in (0, 1):
                        kd = 2 * kp + i
                        tick(sy.dma_start(out=XF[:, kd * TT:(kd + 1) * TT],
                                          in_=x_d[kd, :, tt * TT:(tt + 1) * TT]),
                             xk, 16)
                    marks.append((xk, cnt[xk]))
            return marks

        LNST = {}
        AT = {}
        xb0 = emit_xb_dma(0, [("wg", wg_done)])
        LNST[0] = emit_ln_stats([("wg", xb0), ("v", init_v)], XB, 0)
        xm0 = emit_xload(0, None, init_waits=[("w", w_done), ("v", init_v)])
        xw0 = [[m] for m in xm0]
        AT[0] = (emit_ln_tail("q", 0, LNST[0], XF, pair_waits=xw0),
                 emit_ln_tail("q", 1, LNST[0], XF, pair_waits=xw0))

        for tt in range(NTILE):
            if tt + 1 < NTILE:
                xb_n = emit_xb_dma(tt + 1, [("t", xb_guard[0])])
                def hook(tn=tt + 1, xbn=xb_n):
                    LNST[tn] = emit_ln_stats_a([("wg", xbn)], XB, 0)
                def hook2(tn=tt + 1):
                    emit_ln_stats_b(LNST[tn])
            else:
                hook = hook2 = None
            ffn1_v = emit_ffn(tt, AT[tt][1], XF, True, pump=pump, mid_hook=hook,
                              mid_hook2=hook2)
            pump_all()
            st2 = emit_ln_stats([("v", ffn1_v), ("t", xb_guard[0])], XB, 1,
                                copy_src=XF)
            b0 = emit_ln_tail("ln2", 0, st2, XF, apply_waits=[("sp", rp_prev[0])])
            pt = [0] * (KD // 2)
            b1 = emit_ln_tail("ln2", 1, st2, XF, apply_waits=[("sp", rp_prev[0])],
                              pair_ticks=pt)
            pending.append(quad_stages_p1(tt, b0, b1))
            if tt + 1 < NTILE:
                xm = emit_xload(tt + 1, pt)
                xw = [[m] for m in xm]
                AT[tt + 1] = (emit_ln_tail("q", 0, LNST[tt + 1], XF, pair_waits=xw),
                              emit_ln_tail("q", 1, LNST[tt + 1], XF, pair_waits=xw))
            pump_n(25)
            if tt == NTILE - 1:
                pump_all()

        ha_free = {"t": cnt["t"], "v": cnt["v"]}
        @blk.gpsimd
        def _(g):
            g.wait_ge(s_s, span_s[0])
            g.wait_ge(s_sp, cnt["sp"])
            tick(g.dma_start(out=ccin_d[:, 0:DS], in_=SPB[:]), "sp", 16)
            tick(g.dma_start(out=ccin_d[:, DS:DS + D], in_=SCTB[:]), "sp", 16)
        cc_staged = cnt["sp"]

        gbs = [GB, ZH[0:DS, 0:DS + D]]
        pbls = [QH[0:DS, 0:DS], QH[0:DS, DS:2 * DS]]
        cbls = [CTC, SCTB]
        g_marks = {}
        @blk.gpsimd
        def _(g):
            g.wait_ge(s_sp, cc_staged)
            if sim1:
                for p in range(W):
                    g.dma_start(out=ccout_d[p * DS:(p + 1) * DS, :],
                                in_=ccin_d[:]).then_inc(s_cc, 16)
            else:
                g.collective_compute(
                    "AllGather", OP.bypass,
                    replica_groups=[list(range(W))],
                    ins=[ccin_d.ap().opt()],
                    outs=[ccout_d.ap().opt()],
                ).then_inc(s_cc, 1)
            g.wait_ge(s_cc, 16 * W if sim1 else 1)
            tick(g.dma_start(out=gbs[0][:], in_=ccout_d[0:DS, :]), "g", 16)
            tick(g.dma_start(out=gbs[1][:], in_=ccout_d[DS:2 * DS, :]), "g2", 16)
        g_marks[0] = ("g", cnt["g"])
        g_marks[1] = ("g2", cnt["g2"])

        @blk.vector
        def _(v):
            tick(v.memset(ZSB[:], 0.0), "v")
        z_prev = ("v", cnt["v"])
        bl = {}
        for p in range(W):
            sl = p % 2
            @blk.vector
            def _(v, p=p, sl=sl):
                v.wait_ge(SEMS[g_marks[p][0]], g_marks[p][1])
                if p >= 2:
                    v.wait_ge(s_t, bl[("t_z", p - 2)])
                v.scalar_tensor_tensor(pbls[sl][:], gbs[sl][:, 0:DS],
                                       LAM[:, p:p + 1],
                                       LAMI[:, p * DS:(p + 1) * DS],
                                       OP.mult, OP.add)
                tick(v.tensor_scalar(cbls[sl][:], gbs[sl][:, DS:DS + D],
                                     LAM[:, p:p + 1], None, OP.mult), "v")
            bl[("v", p)] = cnt["v"]
            if p + 2 < W:
                gk = "g" if sl == 0 else "g2"
                @blk.gpsimd
                def _(g, p=p, sl=sl, gk=gk):
                    g.wait_ge(s_v, bl[("v", p)])
                    tick(g.dma_start(out=gbs[sl][:],
                                     in_=ccout_d[(p + 2) * DS:(p + 3) * DS, :]),
                         gk, 16)
                g_marks[p + 2] = (gk, cnt[gk])
            @blk.tensor
            def _(t, p=p, sl=sl):
                t.wait_ge(s_v, bl[("v", p)])
                t.wait_ge(SEMS[z_prev[0]], z_prev[1])
                t.matmul(pC[0:DS, 0:512], pbls[sl][:], ZSB[:, 0:512],
                         start=True, stop=False)
                t.matmul(pC[0:DS, 0:512], I64H[:], cbls[sl][:, 0:512],
                         start=False, stop=True)
                t.matmul(pC[0:DS, 512:1024], pbls[sl][:], ZSB[:, 512:1024],
                         start=True, stop=False)
                tick(t.matmul(pC[0:DS, 512:1024], I64H[:], cbls[sl][:, 512:1024],
                              start=False, stop=True), "t")
            bl[("t_z", p)] = cnt["t"]
            @blk.scalar
            def _(s, p=p):
                s.wait_ge(s_t, bl[("t_z", p)])
                s.activation(ZSB[:, 0:512], pC[0:DS, 0:512], AF.Copy,
                             bias=0.0, scale=1.0)
                tick(s.activation(ZSB[:, 512:1024], pC[0:DS, 512:1024], AF.Copy,
                                  bias=0.0, scale=1.0), "s")
            z_prev = ("s", cnt["s"])

        def zslot(tt, ch):
            if ch == 0:
                return ZSB if tt % 2 == 0 else ZC1
            return ZH[:, (ch - 1) * D: ch * D]

        ct_consumed = [0, 0]
        zstate = {"prev": z_prev}

        def chain_stages_p2(tt, ch):
            cidx = tt * NCH + ch
            slot = (cidx % 2) * D
            rckey = "rc0" if cidx % 2 == 0 else "rc1"
            src = zslot(tt, ch)
            dst = zslot(tt, ch + 1) if ch + 1 < NCH else zslot(tt + 1, 0)
            st = {}
            zp = zstate["prev"]
            @blk.sync
            def _(sy):
                sy.wait_ge(s_t, ct_consumed[cidx % 2])
                if tt == 0 and ch == 0:
                    sy.wait_ge(SEMS[zp[0]], zp[1])
                tick(sy.dma_start(out=CTL[:, slot:slot + D], in_=spct_d[cidx]), rckey, 16)
            r_ct = (rckey, cnt[rckey])
            yield
            @blk.tensor
            def _(t):
                t.wait_ge(SEMS[zp[0]], zp[1])
                t.wait_ge(SEMS[r_ct[0]], r_ct[1])
                pcs = PC[:, cidx * DS:(cidx + 1) * DS]
                t.matmul(pC[0:DS, 0:512], pcs, src[:, 0:512], start=True, stop=False)
                t.matmul(pC[0:DS, 0:512], I64H[:], CTL[:, slot:slot + 512],
                         start=False, stop=True)
                t.matmul(pC[0:DS, 512:1024], pcs, src[:, 512:1024], start=True, stop=False)
                tick(t.matmul(pC[0:DS, 512:1024], I64H[:], CTL[:, slot + 512:slot + D],
                              start=False, stop=True), "t")
            ct_consumed[cidx % 2] = cnt["t"]
            st["t_zc"] = cnt["t"]
            yield
            @blk.scalar
            def _(s):
                s.wait_ge(s_t, st["t_zc"])
                s.activation(dst[:, 0:512], pC[0:DS, 0:512], AF.Copy, bias=0.0, scale=1.0)
                tick(s.activation(dst[:, 512:1024], pC[0:DS, 512:1024], AF.Copy,
                                  bias=0.0, scale=1.0), "s")
            zstate["prev"] = ("s", cnt["s"])


        def emit_ycorr(tt, r_loaded, prev_ffn_v):
            marks = {}
            t_pd_free = cnt["t"]
            vm_prev = {0: None, 1: None}
            for ch in range(NCH):
                csl = [slice(kd * TT + ch * CH, kd * TT + (ch + 1) * CH)
                       for kd in range(KD)]
                src = zslot(tt, ch)
                for wave in range(2):
                    w = ch * 2 + wave
                    bank = pD if w % 2 == 0 else pC
                    @blk.tensor
                    def _(t, src=src, ch=ch, wave=wave, w=w, bank=bank):
                        if w == 0:
                            t.wait_ge(s_r, r_loaded)
                            t.wait_ge(SEMS[zstate["prev"][0]], zstate["prev"][1])
                            t.wait_ge(s_v, cnt["v"])
                        elif vm_prev[w % 2] is not None:
                            t.wait_ge(s_v, vm_prev[w % 2])
                        for j in range(4):
                            kd = wave * 4 + j
                            i = t.matmul(bank[:, j * 128:(j + 1) * 128],
                                         src[:, kd * 128:(kd + 1) * 128],
                                         QHL[:, ch * CH:(ch + 1) * CH],
                                         start=True, stop=True)
                        tick(i, "t")
                    t_y = cnt["t"]
                    @blk.vector
                    def _(v, csl=csl, ch=ch, wave=wave, w=w, bank=bank, t_y=t_y):
                        v.wait_ge(s_t, t_y)
                        if w == 0:
                            v.wait_ge(s_v, prev_ffn_v)
                            v.wait_ge(s_sp, rp_tile_spill[NTILE - 1])
                        for j in range(4):
                            kd = wave * 4 + j
                            i = v.tensor_tensor(RP[:, csl[kd]], HA[:, csl[kd]],
                                                bank[:, j * 128:(j + 1) * 128],
                                                OP.add)
                        tick(i, "v")
                    vm_prev[w % 2] = cnt["v"]
                    marks["x2"] = cnt["v"]
            return marks["x2"]

        prev_ffn2_v = [0]
        prev_out = [0]
        p2_reload = [0] * NTILE

        def emit_reload(tt):
            @blk.sync
            def _(sy):
                sy.wait_ge(s_sp, rp_tile_spill[tt])
                sy.wait_ge(s_r, cnt["r"])
                if tt > 0:
                    sy.wait_ge(s_v, p2_reload_guard[0])
                else:
                    sy.wait_ge(s_t, ha_free["t"])
                    sy.wait_ge(s_v, ha_free["v"])
                for kd in range(KD):
                    tick(sy.dma_start(out=HA[:, kd * TT:(kd + 1) * TT],
                                      in_=sprp_d[kd, :, tt * TT:(tt + 1) * TT]), "r", 16)
                for ch in range(NCH):
                    tick(sy.dma_start(out=QHL[:, ch * CH:(ch + 1) * CH],
                                      in_=spqh_d[tt * NCH + ch]), "r", 16)
            p2_reload[tt] = cnt["r"]

        p2_reload_guard = [0]
        emit_reload(0)
        for ch in range(NCH):
            pending.append(chain_stages_p2(0, ch))
        pump_all()

        for tt in range(NTILE):
            x2_v = emit_ycorr(tt, p2_reload[tt], prev_ffn2_v[0])
            p2_reload_guard[0] = x2_v
            st3 = emit_ln_stats([("v", x2_v)], RP, 0, pump=pump)
            c0 = emit_ln_tail("q", 0, st3, RP)
            if tt + 1 < NTILE:
                emit_reload(tt + 1)
                for ch in range(NCH):
                    pending.append(chain_stages_p2(tt + 1, ch))
            c1 = emit_ln_tail("q", 1, st3, RP)
            ffn2_v = emit_ffn(tt, c1, RP, False,
                              evac_waits=[("o", prev_out[0])], pump=pump)
            pump_all()
            prev_ffn2_v[0] = ffn2_v


            @blk.sync
            def _(sy):
                sy.wait_ge(s_v, ffn2_v)
                for kd in range(KD):
                    tick(sy.dma_start(out=out_d[kd, :, tt * TT:(tt + 1) * TT],
                                      in_=XF[:, kd * TT:(kd + 1) * TT]), "o", 16)
            prev_out[0] = cnt["o"]

        @blk.sync
        def _(sy):
            sy.wait_ge(s_o, cnt["o"])
    return nc


def threefry2x32_np(k0, k1, c0, c1):
    ks0, ks1 = np.uint32(k0), np.uint32(k1)
    ks2 = np.uint32(ks0 ^ ks1 ^ np.uint32(0x1BD11BDA))
    rot = [[13, 15, 26, 6], [17, 29, 16, 24]]
    def rotl(x, d):
        return ((x << np.uint32(d)) | (x >> np.uint32(32 - d))).astype(np.uint32)
    x0 = (c0 + ks0).astype(np.uint32); x1 = (c1 + ks1).astype(np.uint32)
    ks = [ks0, ks1, ks2]
    for i in range(5):
        r = rot[0] if i % 2 == 0 else rot[1]
        for j in range(4):
            x0 = (x0 + x1).astype(np.uint32)
            x1 = rotl(x1, r[j])
            x1 = (x1 ^ x0).astype(np.uint32)
        x0 = (x0 + ks[(i + 1) % 3]).astype(np.uint32)
        x1 = (x1 + ks[(i + 2) % 3] + np.uint32(i + 1)).astype(np.uint32)
    return x0, x1


def host_prep(inp):
    x = np.ascontiguousarray(np.asarray(inp["L0_attn_x"], np.float32))
    B, S, Dm = x.shape
    xf = x.reshape(B * S, Dm)
    n = B * S * Dm
    c2 = np.arange(n, dtype=np.uint32)
    c1 = np.zeros(n, np.uint32)
    a0, a1 = threefry2x32_np(0, 42, c1, c2)
    bits = (a0 ^ a1).astype(np.uint32)
    f = ((bits >> np.uint32(9)) | np.uint32(0x3F800000)).view(np.float32) - np.float32(1.0)
    keep = np.maximum(np.float32(0.0), f) < np.float32(0.8)
    mask = np.where(keep.reshape(B * S, Dm), np.float16(0.625), np.float16(0.0))

    w1 = np.asarray(inp["L0_ffn_w1"], np.float32)
    w2 = np.asarray(inp["L0_ffn_w2"], np.float32)
    wv = np.asarray(inp["L0_attn_Wv"], np.float32)
    wk = np.asarray(inp["L0_attn_Wk"], np.float32)
    wq = np.asarray(inp["L0_attn_Wq"], np.float32)
    w1t = np.ascontiguousarray(w1.T.reshape(KD, 128, Hh))
    w2t = np.ascontiguousarray(w2.T.reshape(KH, 128, D))
    wvt = np.ascontiguousarray(wv.T.reshape(KD, 128, D))
    wkt = np.ascontiguousarray(wk.T.reshape(KD, 128, DS))
    wqt = np.ascontiguousarray(wq.T.reshape(KD, 128, DS))
    bias_h = np.ascontiguousarray(np.asarray(inp["L0_ffn_b1"], np.float32)
                                  .reshape(KH, 128).T)
    vecd = np.zeros((128, 24), np.float32)
    vecd[:, 0:8] = np.asarray(inp["L0_ffn_b2"], np.float32).reshape(KD, 128).T
    vecd[:, 8:16] = np.asarray(inp["L0_attn_gamma"], np.float32).reshape(KD, 128).T
    vecd[:, 16:24] = np.asarray(inp["L0_attn_beta"], np.float32).reshape(KD, 128).T
    bias_e = np.stack([np.asarray(inp["L0_attn_bk"], np.float32),
                       np.asarray(inp["L0_attn_bq"], np.float32)], axis=1)
    bvrow = np.asarray(inp["L0_attn_bv"], np.float32).reshape(1, D)
    bkrow = np.asarray(inp["L0_attn_bk"], np.float32).reshape(1, DS)
    consts = np.zeros((128, 576), np.float32)
    consts[:, 0:128] = np.tril(np.ones((128, 128), np.float32), -1)
    consts[:, 128:256] = np.triu(np.ones((128, 128), np.float32), 1)
    consts[:, 256:384] = 2.0 * np.eye(128, dtype=np.float32)
    consts[:, 384:512] = np.eye(128, dtype=np.float32)
    consts[0:DS, 512:576] = np.eye(DS, dtype=np.float32)
    i64h = np.eye(DS, dtype=np.float16)

    shared = dict(w1t=w1t, w2t=w2t, wvt=wvt, wkt=wkt, wqt=wqt, bias_h=bias_h,
                  vecd=vecd, bias_e=bias_e, bvrow=bvrow, bkrow=bkrow,
                  consts=consts, i64h=i64h)
    in_maps = []
    spanlen = B * S // W
    for c in range(W):
        sl = slice(c * spanlen, (c + 1) * spanlen)
        xs = np.ascontiguousarray(xf[sl].T.reshape(KD, 128, SPAN))
        ms = np.ascontiguousarray(mask[sl].T.reshape(KD, 128, SPAN))
        lam = np.zeros((DS, W), np.float32)
        g0 = 4 * (c // 4)
        for p in range(g0, c):
            lam[:, p] = 1.0
        lamI = np.zeros((W, DS, DS), np.float16)
        for p in range(W):
            lamI[p] = np.eye(DS, dtype=np.float16) * (0.0 if g0 <= p < c else 1.0)
        lamI = np.ascontiguousarray(lamI.transpose(1, 0, 2).reshape(DS, W * DS))
        m = dict(shared)
        m["x"] = xs
        m["mask"] = ms
        m["lam"] = lam
        m["lamI"] = lamI
        in_maps.append(m)
    return in_maps


def gather_out(results):
    outs = []
    for c in range(W):
        o = results[c]["out"]
        outs.append(np.ascontiguousarray(o.reshape(D, SPAN).T))
    return np.concatenate(outs, axis=0)


def kernel(**inputs):
    import concourse.bass_utils as bass_utils
    nc = build_program("full")
    in_maps = host_prep(inputs)
    res = bass_utils.run_bass_kernel_spmd(nc, in_maps, core_ids=list(range(W)))
    out = gather_out(res.results)
    return out.reshape(2, 16384, 1024)
